# version 08
# speedup vs baseline: 1.0618x; 1.0002x over previous
import math
from contextlib import ExitStack

import numpy as np
import concourse.bass as bass
import concourse.mybir as mybir
from concourse.bass_utils import run_bass_kernel_spmd

F32 = mybir.dt.float32
BF16 = mybir.dt.bfloat16
AF = mybir.ActivationFunctionType
ALU = mybir.AluOpType
AX = mybir.AxisListType

SEQ = 2048
D = 1024
NT = SEQ // 128
ND = D // 128


class P:

    def __init__(self, nc, ctx):
        self.nc = nc
        self.ctx = ctx
        self.root = ctx
        self.sem = {}
        self.cnt = {}
        self.buf = {}
        self.mkcnt = 0
        self.marks = []
        self.mkbuf = self.root.enter_context(
            nc.sbuf_tensor("mkbuf", [1, 8], F32))

    def _pool(self, i):
        pl = getattr(self, "pl", None)
        if pl is None:
            self.pl, self.pcnt = [], []
            pl = self.pl
        while len(pl) <= i:
            pl.append(self.root.enter_context(
                self.nc.semaphore(f"sp{len(pl)}")))
            self.pcnt.append(0)
        return pl[i]

    def sems(self, *names):
        if not hasattr(self, "slot"):
            self.slot = {}
        for i, n in enumerate(names):
            h = self._pool(i)
            self.slot[n] = i
            self.sem[n] = h
            self.cnt[n] = self.pcnt[i]

    def inc(self, inst, name, k=1):
        i = self.slot[name]
        self.pcnt[i] += k
        self.cnt[name] = self.pcnt[i]
        inst.then_inc(self.sem[name], k)
        return inst

    def base(self, *names):
        self.sems(*names)
        return {n: self.cnt[n] for n in names}

    def sb(self, name, shape, dtype):
        self.uid = getattr(self, "uid", 0) + 1
        t = self.ctx.enter_context(
            self.nc.sbuf_tensor(f"{name}__{self.uid}", shape, dtype))
        self.buf[name] = t
        return t

    def mark(self, tag):
        nc = self.nc
        if not hasattr(self, "mksem"):
            self.mksem = self.root.enter_context(nc.semaphore("mark"))
        self.marks.append(tag)
        self.mkcnt += 1
        tgt = self.mkcnt * 16
        mb = self.mkbuf
        with nc.Block() as blk:
            @blk.sync
            def _(s):
                s.dma_start(out=mb[0:1, 4:8],
                            in_=mb[0:1, 0:4]).then_inc(self.mksem, 16)
                s.wait_ge(self.mksem, tgt)

    def ps(self, name, shape, dtype=F32):
        t = self.ctx.enter_context(self.nc.psum_tensor(name, shape, dtype))
        self.buf[name] = t
        return t

    from contextlib import contextmanager as _cm

    @_cm
    def scope(self):
        saved = self.ctx
        with ExitStack() as inner:
            self.ctx = inner
            try:
                yield
            finally:
                self.ctx = saved


def phase_ln_transpose_in(p, X_ap, eps=1e-5):
    nc = p.nc
    B = p.buf
    xnT, xt, xn, junk, cols = B["xnT"], B["xt2"], B["xn2"], B["junk"], B["cols"]
    idbf = B["idbf"]
    pst = [B["psTA"], B["psTB"]]

    b = p.base("i_dx", "i_vs", "i_sq", "i_nm", "i_tt", "i_ev")
    xnT_v = xnT[:, :].rearrange("p (j s) -> p j s", j=ND, s=SEQ)

    def evac(v, i):
        v.wait_ge(p.sem["i_tt"], b["i_tt"] + i + 1)
        ps = pst[i % 2]
        src = ps[:, 0:ND * 128].rearrange("p (j c) -> p j c", j=ND, c=128)
        dst = xnT_v[:, :, 128 * i:128 * i + 128]
        p.inc(v.tensor_copy(dst, src), "i_ev")

    with nc.Block() as blk:
        @blk.sync
        def _(s):
            for i in range(NT):
                if i >= 2:
                    s.wait_ge(p.sem["i_nm"], b["i_nm"] + i - 1)
                p.inc(s.dma_start(out=xt[:, (i % 2) * D:(i % 2) * D + D],
                                  in_=X_ap[128 * i:128 * i + 128, :]), "i_dx", 16)

        @blk.vector
        def _(v):
            for i in range(NT):
                if i >= 1:
                    v.wait_ge(p.sem["i_nm"], b["i_nm"] + i)
                v.wait_ge(p.sem["i_dx"], b["i_dx"] + 16 * (i + 1))
                xti = xt[:, (i % 2) * D:(i % 2) * D + D]
                v.reduce_sum(cols[:, 0:1], xti, axis=AX.X)
                v.wait_ge(p.sem["i_sq"], b["i_sq"] + i + 1)
                v.drain()
                v.tensor_scalar_mul(cols[:, 2:3], cols[:, 0:1], 1.0 / D)
                v.drain()
                v.tensor_mul(cols[:, 3:4], cols[:, 2:3], cols[:, 2:3])
                v.tensor_scalar(cols[:, 4:5], cols[:, 1:2], 1.0 / D, None, ALU.mult)
                v.drain()
                v.tensor_sub(cols[:, 5:6], cols[:, 4:5], cols[:, 3:4])
                v.drain()
                v.tensor_scalar_add(cols[:, 5:6], cols[:, 5:6], eps)
                v.drain()
                v.reciprocal(cols[:, 6:7], cols[:, 5:6])
                p.inc(v.drain(), "i_vs")
                if i >= 1:
                    evac(v, i - 1)
            evac(v, NT - 1)

        @blk.scalar
        def _(sc):
            for i in range(NT):
                sc.wait_ge(p.sem["i_dx"], b["i_dx"] + 16 * (i + 1))
                if i >= 1:
                    sc.wait_ge(p.sem["i_vs"], b["i_vs"] + i)
                xti = xt[:, (i % 2) * D:(i % 2) * D + D]
                sc.activation(junk[:, :], xti, AF.Square,
                              accum_out=cols[:, 1:2])
                p.inc(sc.drain(), "i_sq")
                sc.wait_ge(p.sem["i_vs"], b["i_vs"] + i + 1)
                sc.activation(cols[:, 7:8], cols[:, 6:7], AF.Sqrt)
                sc.mul(cols[:, 8:9], cols[:, 2:3], -1.0)
                sc.drain()
                sc.activation(cols[:, 9:10], cols[:, 8:9], AF.Identity,
                              scale=cols[:, 7:8])
                sc.drain()
                if i >= 2:
                    sc.wait_ge(p.sem["i_tt"], b["i_tt"] + i - 1)
                p.inc(sc.activation(xn[:, (i % 2) * D:(i % 2) * D + D], xti,
                                    AF.Identity, scale=cols[:, 7:8],
                                    bias=cols[:, 9:10]), "i_nm")

        @blk.tensor
        def _(t):
            for i in range(NT):
                t.wait_ge(p.sem["i_nm"], b["i_nm"] + i + 1)
                if i >= 2:
                    t.wait_ge(p.sem["i_ev"], b["i_ev"] + i - 1)
                ps = pst[i % 2]
                for j in range(ND):
                    ins = t.matmul(ps[:, 128 * j:128 * j + 128],
                                   xn[:, (i % 2) * D + 128 * j:
                                       (i % 2) * D + 128 * j + 128],
                                   idbf[:, :], start=True, stop=True)
                    if j == ND - 1:
                        p.inc(ins, "i_tt")


def phase_residual_out(p, X_old_ap, X_new_ap, out_buf, tile0, ntiles, eps=1e-5):
    nc = p.nc
    B = p.buf
    xo, tmp, junk, cols = B["xt2"], B["tmp_f32"], B["junk"], B["cols"]
    b = p.base("r_dx", "r_sq", "r_vs", "r_cb", "r_st", "r_ad")

    with nc.Block() as blk:
        @blk.sync
        def _(s):
            for i in range(ntiles):
                if i >= 2:
                    s.wait_ge(p.sem["r_st"], b["r_st"] + 16 * (i - 1))
                r0 = 128 * (tile0 + i)
                p.inc(s.dma_start(out=xo[:, (i % 2) * D:(i % 2) * D + D],
                                  in_=X_old_ap[r0:r0 + 128, :]), "r_dx", 16)
                if i >= 1:
                    s.wait_ge(p.sem["r_ad"], b["r_ad"] + i)
                    r1 = 128 * (tile0 + i - 1)
                    p.inc(s.dma_start(
                        out=X_new_ap[r1:r1 + 128, :],
                        in_=xo[:, ((i - 1) % 2) * D:((i - 1) % 2) * D + D]),
                        "r_st", 16)
            s.wait_ge(p.sem["r_ad"], b["r_ad"] + ntiles)
            r1 = 128 * (tile0 + ntiles - 1)
            p.inc(s.dma_start(
                out=X_new_ap[r1:r1 + 128, :],
                in_=xo[:, ((ntiles - 1) % 2) * D:((ntiles - 1) % 2) * D + D]),
                "r_st", 16)
            s.wait_ge(p.sem["r_st"], b["r_st"] + 16 * ntiles)

        @blk.scalar
        def _(sc):
            for i in range(ntiles):
                if i >= 1:
                    sc.wait_ge(p.sem["r_vs"], b["r_vs"] + i)
                ot = out_buf[:, i * D:i * D + D]
                sc.activation(junk[:, :], ot, AF.Square,
                              accum_out=cols[:, 1:2])
                p.inc(sc.drain(), "r_sq")
                sc.wait_ge(p.sem["r_vs"], b["r_vs"] + i + 1)
                sc.activation(cols[:, 7:8], cols[:, 6:7], AF.Sqrt)
                sc.mul(cols[:, 8:9], cols[:, 2:3], -1.0)
                sc.drain()
                sc.activation(cols[:, 9:10], cols[:, 8:9], AF.Identity,
                              scale=cols[:, 7:8])
                sc.drain()
                p.inc(sc.activation(tmp[:, :], ot, AF.Identity, scale=cols[:, 7:8],
                                    bias=cols[:, 9:10]), "r_cb")

        @blk.vector
        def _(v):
            for i in range(ntiles):
                if i >= 1:
                    v.wait_ge(p.sem["r_cb"], b["r_cb"] + i)
                ot = out_buf[:, i * D:i * D + D]
                v.reduce_sum(cols[:, 0:1], ot, axis=AX.X)
                v.wait_ge(p.sem["r_sq"], b["r_sq"] + i + 1)
                v.drain()
                v.tensor_scalar_mul(cols[:, 2:3], cols[:, 0:1], 1.0 / D)
                v.drain()
                v.tensor_mul(cols[:, 3:4], cols[:, 2:3], cols[:, 2:3])
                v.tensor_scalar(cols[:, 4:5], cols[:, 1:2], 1.0 / D, None, ALU.mult)
                v.drain()
                v.tensor_sub(cols[:, 5:6], cols[:, 4:5], cols[:, 3:4])
                v.drain()
                v.tensor_scalar_add(cols[:, 5:6], cols[:, 5:6], eps)
                v.drain()
                v.reciprocal(cols[:, 6:7], cols[:, 5:6])
                p.inc(v.drain(), "r_vs")
                v.wait_ge(p.sem["r_cb"], b["r_cb"] + i + 1)
                v.wait_ge(p.sem["r_dx"], b["r_dx"] + 16 * (i + 1))
                xoi = xo[:, (i % 2) * D:(i % 2) * D + D]
                p.inc(v.tensor_add(xoi, xoi, tmp[:, :]), "r_ad")


def alloc_common(p):
    p.sb("xnT", [128, ND * SEQ], BF16)
    p.sb("xt2", [128, 2 * D], F32)
    p.sb("tmp_f32", [128, D], F32)
    p.sb("cols", [128, 16], F32)
    U = p.sb("U16", [128, 4 * D], F32)
    p.buf["out_attn"] = U[:, :]
    p.buf["wg"] = U[:, :].bitcast(BF16)[:, 0:4096]
    p.buf["xn2"] = U[:, :].bitcast(BF16)[:, 4096:6144]
    S2 = p.sb("scr2k", [128, D], BF16)
    p.buf["junk"] = S2[:, :]
    p.buf["pt0"] = S2[:, 0:512]
    p.buf["pt1"] = S2[:, 512:1024]
    p.sb("idbf", [128, 128], BF16)
    p.sb("idf32", [1, 4], F32)
    p.sb("iota_row", [1, SEQ], F32)
    p.sb("iotac", [128, 1], F32)
    p.sb("cst", [128, 4], F32)
    p.ps("psA", [128, 512], F32)
    p.ps("psB", [128, 512], F32)
    p.ps("psC", [128, 512], F32)
    p.ps("psTA", [128, 1024], F32)
    p.ps("psTB", [128, 1024], F32)
    p.ps("psZ", [128, 128], F32)


def alloc_sb2(p):
    alloc_common(p)
    alloc_sb2_extra(p)


def alloc_sb2_extra(p):
    p.sb("kx", [128, 16 * SEQ], BF16)
    p.sb("qx", [128, 16 * 1024], BF16)
    p.sb("vx", [128, NT * 16 * 65], BF16)
    p.sb("ctxe", [65, 512], BF16)
    p.sb("zr", [128, 8], F32)
    p.sb("rzrow", [1, 512], F32)
    p.sb("mask", [128, 128], BF16)
    p.sb("m_rep", [128, 16], F32)
    p.sb("m_row", [1, 16], F32)
    p.sb("bias_cols", [128, 256], F32)


def alloc_sb0_extra(p):
    p.sb("b_qT", [128, 8 * SEQ], BF16)
    p.sb("b_kT", [128, 8 * SEQ], BF16)
    p.sb("b_vx", [128, NT * 8 * 130], BF16)
    p.sb("b_wo", [128, 8 * 1024], BF16)
    p.sb("b_c0lo", [64, 512], BF16)
    p.sb("b_c0hi", [64, 512], BF16)
    p.sb("b_c1lo", [64, 512], BF16)
    p.sb("b_c1hi", [64, 512], BF16)
    p.sb("b_z0", [1, 512], F32)
    p.sb("b_z1", [1, 512], F32)
    p.sb("b_zrs", [128, 8], F32)
    p.sb("b_r1l", [128, 4], F32)
    p.sb("b_nlam", [128, 1], F32)
    p.sb("b_nlam4", [128, 4], F32)
    p.sb("b_obuf", [128, 512], F32)
    p.sb("b_ubuf", [128, 512], F32)
    p.sb("b_oln", [128, 512], BF16)
    p.sb("b_olnT", [128, 8 * 512], BF16)
    p.sb("b_grep", [128, 128], F32)
    p.sb("b_brep", [128, 128], F32)
    p.sb("b_lr", [4, 64], F32)
    p.sb("b_l0", [1, 64], F32)
    p.sb("b_l1", [1, 64], F32)
    p.sb("b_l2", [1, 64], F32)
    p.sb("b_l3", [1, 64], F32)
    p.sb("b_sc", [1, 8], F32)


def phase_sb2_consts(p, aps):
    nc = p.nc
    B = p.buf
    b = p.base("c_dc")
    with nc.Block() as blk:
        @blk.gpsimd
        def _(g):
            p.inc(g.dma_start(out=B["idbf"][:, :], in_=aps["C_EYE"]), "c_dc", 16)

        @blk.gpsimd
        def _(g2):
            p.inc(g2.dma_start(out=B["mask"][:, :], in_=aps["C_MASK"]), "c_dc", 16)

        @blk.vector
        def _(v):
            v.memset(B["idf32"][:, :], 1.0)

        @blk.sync
        def _(s):
            p.inc(s.dma_start(out=B["iota_row"][:, :], in_=aps["C_IOTA"]), "c_dc", 16)
            p.inc(s.dma_start(out=B["iotac"][:, :], in_=aps["C_IOTAC"]), "c_dc", 16)
            mr = aps["m"].rearrange("h a b -> (a b) h")
            p.inc(s.dma_start(out=B["m_row"][:, :], in_=mr), "c_dc", 16)
            p.inc(s.dma_start(out=B["m_rep"][:, :],
                              in_=mr.to_broadcast([128, 16])), "c_dc", 16)
            s.wait_ge(p.sem["c_dc"], b["c_dc"] + 6 * 16)


def phase_sb2_kv_proj(p, aps):
    nc = p.nc
    B = p.buf
    xnT, kx, vx, wg = B["xnT"], B["kx"], B["vx"], B["wg"]
    psmm = [B["psA"], B["psB"]]
    Wk = aps["W_kqv"]
    Wv = Wk.rearrange("(a p) n -> p a n", p=128)
    vx_v = vx[:, :].rearrange("p (t h c) -> p t h c", t=NT, h=16, c=65)

    for half in range(2):
        b = p.base("k_dw", "k_mm", "k_ev")
        with nc.Block() as blk:
            @blk.gpsimd
            def _(g):
                src = Wv[:, :, 512 * half:512 * half + 512]
                p.inc(g.dma_start(out=wg[:, :].rearrange("p (a n) -> p a n", a=8),
                                  in_=src), "k_dw", 16)

            @blk.tensor
            def _(t):
                t.wait_ge(p.sem["k_dw"], b["k_dw"] + 16)
                n = 0
                for h in range(8 * half, 8 * half + 8):
                    hl = h - 8 * half
                    for c in range(4):
                        ps = psmm[n % 2]
                        if n >= 2:
                            t.wait_ge(p.sem["k_ev"], b["k_ev"] + n - 1)
                        for kd in range(ND):
                            ins = t.matmul(
                                ps[0:64, 0:512],
                                wg[:, kd * 512 + 64 * hl:kd * 512 + 64 * hl + 64],
                                xnT[:, kd * SEQ + 512 * c:kd * SEQ + 512 * c + 512],
                                start=(kd == 0), stop=(kd == ND - 1))
                        p.inc(ins, "k_mm")
                        n += 1

            @blk.scalar
            def _(sc):
                n = 0
                for h in range(8 * half, 8 * half + 8):
                    for c in range(4):
                        ps = psmm[n % 2]
                        sc.wait_ge(p.sem["k_mm"], b["k_mm"] + n + 1)
                        p.inc(sc.activation(
                            kx[0:64, SEQ * h + 512 * c:SEQ * h + 512 * c + 512],
                            ps[0:64, 0:512], AF.Copy), "k_ev")
                        n += 1

    for half in range(2):
        b = p.base("v_dw", "v_mm", "v_ev")
        with nc.Block() as blk:
            @blk.gpsimd
            def _(g):
                src = Wv[:, :, 2048 + 512 * half:2048 + 512 * half + 512]
                p.inc(g.dma_start(out=wg[:, :].rearrange("p (a n) -> p a n", a=8),
                                  in_=src), "v_dw", 16)

            @blk.tensor
            def _(t):
                t.wait_ge(p.sem["v_dw"], b["v_dw"] + 16)
                for tt in range(NT):
                    ps = psmm[tt % 2]
                    if tt >= 2:
                        t.wait_ge(p.sem["v_ev"], b["v_ev"] + tt - 1)
                    for kd in range(ND):
                        ins = t.matmul(
                            ps[:, 0:512],
                            xnT[:, kd * SEQ + 128 * tt:kd * SEQ + 128 * tt + 128],
                            wg[:, kd * 512:kd * 512 + 512],
                            start=(kd == 0), stop=(kd == ND - 1))
                    p.inc(ins, "v_mm")

            @blk.scalar
            def _(sc):
                for tt in range(NT):
                    ps = psmm[tt % 2]
                    sc.wait_ge(p.sem["v_mm"], b["v_mm"] + tt + 1)
                    dst = vx_v[:, tt, 8 * half:8 * half + 8, 0:64]
                    p.inc(sc.activation(dst, ps[:, 0:512], AF.Copy), "v_ev")


def phase_sb2_q_proj(p, aps, qhalf):
    nc = p.nc
    B = p.buf
    xnT, qx, wg = B["xnT"], B["qx"], B["wg"]
    psmm = [B["psA"], B["psB"]]
    Wv = aps["W_kqv"].rearrange("(a p) n -> p a n", p=128)
    q0 = 1024 * qhalf

    for half in range(2):
        b = p.base("q_dw", "q_mm", "q_ev")
        with nc.Block() as blk:
            @blk.gpsimd
            def _(g):
                src = Wv[:, :, 1024 + 512 * half:1024 + 512 * half + 512]
                p.inc(g.dma_start(out=wg[:, :].rearrange("p (a n) -> p a n", a=8),
                                  in_=src), "q_dw", 16)

            @blk.tensor
            def _(t):
                t.wait_ge(p.sem["q_dw"], b["q_dw"] + 16)
                n = 0
                for h in range(8 * half, 8 * half + 8):
                    hl = h - 8 * half
                    for c in range(2):
                        ps = psmm[n % 2]
                        if n >= 2:
                            t.wait_ge(p.sem["q_ev"], b["q_ev"] + n - 1)
                        for kd in range(ND):
                            ins = t.matmul(
                                ps[0:64, 0:512],
                                wg[:, kd * 512 + 64 * hl:kd * 512 + 64 * hl + 64],
                                xnT[:, kd * SEQ + q0 + 512 * c:kd * SEQ + q0 + 512 * c + 512],
                                start=(kd == 0), stop=(kd == ND - 1))
                        p.inc(ins, "q_mm")
                        n += 1

            @blk.scalar
            def _(sc):
                n = 0
                for h in range(8 * half, 8 * half + 8):
                    for c in range(2):
                        ps = psmm[n % 2]
                        sc.wait_ge(p.sem["q_mm"], b["q_mm"] + n + 1)
                        p.inc(sc.activation(
                            qx[0:64, 1024 * h + 512 * c:1024 * h + 512 * c + 512],
                            ps[0:64, 0:512], AF.Copy, scale=1.0 / 32.0), "q_ev")
                        n += 1

    b = p.base("q_mr")
    with nc.Block() as blk:
        @blk.vector
        def _(v):
            for h in range(16):
                v.tensor_single_scalar(qx[64:65, 1024 * h:1024 * h + 1024],
                                       B["iota_row"][0:1, q0:q0 + 1024],
                                       B["m_row"][0:1, h:h + 1], ALU.mult)
                v.drain()
                v.tensor_scalar(qx[64:65, 1024 * h:1024 * h + 1024],
                                qx[64:65, 1024 * h:1024 * h + 1024],
                                0.0, None, ALU.max)
            p.inc(v.drain(), "q_mr")


def phase_sb2_prep(p):
    nc = p.nc
    B = p.buf
    kx, vx, cols = B["kx"], B["vx"], B["cols"]
    vx_v = vx[:, :].rearrange("p (t h c) -> p t h c", t=NT, h=16, c=65)
    b = p.base("x_pr")
    with nc.Block() as blk:
        @blk.vector
        def _(v):
            v.memset(kx[64:65, :], -1.0)
            v.memset(vx_v[:, :, :, 64:65], 1.0)
            for kt in range(16):
                v.tensor_scalar_add(cols[:, kt:kt + 1], B["iotac"][:, 0:1],
                                    128.0 * kt)
            v.drain()
            for h in range(16):
                v.tensor_scalar(B["bias_cols"][:, 16 * h:16 * h + 16],
                                cols[:, 0:16], B["m_rep"][:, h:h + 1], None,
                                ALU.mult)
            p.inc(v.drain(), "x_pr")


def phase_sb2_attn(p, X_in, X_out, qc):
    nc = p.nc
    B = p.buf
    kx, qx, vx = B["kx"], B["qx"], B["vx"]
    out_attn, ctxe, zr, rzrow = B["out_attn"], B["ctxe"], B["zr"], B["rzrow"]
    pts = [B["pt0"], B["pt1"]]
    psS = [B["psA"], B["psB"]]
    psC, psZ = B["psC"], B["psZ"]
    psT = [B["psTA"], B["psTB"]]
    vx_v = vx[:, :].rearrange("p (t h c) -> p t h c", t=NT, h=16, c=65)
    H = 16
    nkt = 4 * qc + 4
    qloc = 512 * (qc % 2)

    b = p.base("a_ts", "a_vm", "a_se", "a_tc", "a_zr", "a_ce", "a_tz", "a_tt",
               "a_vz", "a_so")

    def masked(kt):
        return kt - 4 * qc

    with nc.Block() as blk:
        @blk.tensor
        def _(t):

            def emit_ctx(i):
                hh, kk = i // nkt, i % nkt
                t.wait_ge(p.sem["a_se"], b["a_se"] + i + 1)
                if kk == 0 and hh >= 1:
                    t.wait_ge(p.sem["a_ce"], b["a_ce"] + hh)
                ins = t.matmul(psC[0:65, 0:512], vx_v[:, kk, hh, :],
                               pts[kk % 2][:, :],
                               start=(kk == 0), stop=(kk == nkt - 1))
                p.inc(ins, "a_tc")

            ctx_next = 0
            for h in range(H):
                for kt in range(nkt):
                    g = h * nkt + kt
                    d = masked(kt)
                    ps = psS[kt % 2]
                    if g >= 2:
                        t.wait_ge(p.sem["a_se"], b["a_se"] + g - 1)
                    off = 128 * d if d > 0 else 0
                    ins = t.matmul(
                        ps[:, off:512],
                        kx[0:65, SEQ * h + 128 * kt:SEQ * h + 128 * kt + 128],
                        qx[0:65, 1024 * h + qloc + off:1024 * h + qloc + 512],
                        start=True, stop=True)
                    p.inc(ins, "a_ts")
                    if ctx_next < g:
                        emit_ctx(ctx_next)
                        ctx_next += 1
                while ctx_next <= h * nkt + nkt - 1:
                    emit_ctx(ctx_next)
                    ctx_next += 1
                t.wait_ge(p.sem["a_zr"], b["a_zr"] + h + 1)
                if h >= 1:
                    t.wait_ge(p.sem["a_vz"], b["a_vz"] + h)
                for j in range(4):
                    ins = t.matmul(psZ[:, j:j + 1],
                                   rzrow[0:1, 128 * j:128 * j + 128],
                                   B["idf32"][0:1, 0:1], start=True, stop=True)
                    if j == 3:
                        p.inc(ins, "a_tz")
                t.wait_ge(p.sem["a_ce"], b["a_ce"] + h + 1)
                if h >= 2:
                    t.wait_ge(p.sem["a_so"], b["a_so"] + 4 * (h - 1))
                for j in range(4):
                    ins = t.matmul(psT[h % 2][:, 128 * j:128 * j + 65],
                                   ctxe[0:65, 128 * j:128 * j + 128],
                                   B["idbf"][0:65, 0:65], start=True, stop=True)
                    p.inc(ins, "a_tt")

        @blk.vector
        def _(v):
            nvm = 0
            for h in range(H):
                for kt in range(nkt):
                    g = h * nkt + kt
                    d = masked(kt)
                    if d >= 0:
                        ps = psS[kt % 2]
                        v.wait_ge(p.sem["a_ts"], b["a_ts"] + g + 1)
                        if d > 0:
                            v.memset(ps[:, 0:128 * d], -1e30)
                        v.tensor_add(ps[:, 128 * d:128 * d + 128],
                                     ps[:, 128 * d:128 * d + 128], B["mask"][:, :])
                        nvm += 1
                        p.inc(v.engine_nop(), "a_vm")
                v.wait_ge(p.sem["a_tc"], b["a_tc"] + (h + 1) * nkt)
                v.reciprocal(rzrow[0:1, :], psC[64:65, 0:512])
                p.inc(v.drain(), "a_zr")
                if h >= 1:
                    v.wait_ge(p.sem["a_tt"], b["a_tt"] + 4 * h)
                v.tensor_copy(ctxe[0:65, :], psC[0:65, 0:512])
                p.inc(v.drain(), "a_ce")
                v.wait_ge(p.sem["a_tz"], b["a_tz"] + h + 1)
                v.tensor_copy(zr[:, 4 * (h % 2):4 * (h % 2) + 4], psZ[:, 0:4])
                p.inc(v.drain(), "a_vz")
                for j in range(4):
                    v.wait_ge(p.sem["a_tt"], b["a_tt"] + 4 * h + j + 1)
                    v.tensor_scalar(
                        out_attn[:, D * j + 64 * h:D * j + 64 * h + 64],
                        psT[h % 2][:, 128 * j:128 * j + 64],
                        zr[:, 4 * (h % 2) + j:4 * (h % 2) + j + 1], None,
                        ALU.mult)
                    p.inc(v.drain(), "a_so")

        @blk.scalar
        def _(sc):
            nvm = 0
            for h in range(H):
                for kt in range(nkt):
                    g = h * nkt + kt
                    d = masked(kt)
                    ps = psS[kt % 2]
                    sc.wait_ge(p.sem["a_ts"], b["a_ts"] + g + 1)
                    if d >= 0:
                        nvm += 1
                        sc.wait_ge(p.sem["a_vm"], b["a_vm"] + nvm)
                    if g >= 2:
                        sc.wait_ge(p.sem["a_tc"], b["a_tc"] + g - 1)
                    p.inc(sc.activation(
                        pts[kt % 2][:, :], ps[:, 0:512], AF.Exp,
                        bias=B["bias_cols"][:, 16 * h + kt:16 * h + kt + 1]),
                        "a_se")

    phase_residual_out(p, X_in, X_out, out_attn, 4 * qc, 4)


def emit_sb2(p, X_in, X_out, aps):
    p.mark("2c")
    phase_sb2_consts(p, aps)
    p.mark("2ln")
    phase_ln_transpose_in(p, X_in)
    p.mark("2kv")
    phase_sb2_kv_proj(p, aps)
    p.mark("2prep")
    phase_sb2_prep(p)
    for qhalf in range(2):
        p.mark(f"2qp{qhalf}")
        phase_sb2_q_proj(p, aps, qhalf)
        for qc in (2 * qhalf, 2 * qhalf + 1):
            p.mark(f"2at{qc}")
            phase_sb2_attn(p, X_in, X_out, qc)


def build_sb2_program():
    nc = bass.Bass("TRN2", target_bir_lowering=False, debug=False)
    X = nc.dram_tensor("X", [SEQ, D], F32, kind="ExternalInput").ap()
    Wk = nc.dram_tensor("W_kqv", [D, 3 * D], F32, kind="ExternalInput").ap()
    m = nc.dram_tensor("m", [16, 1, 1], F32, kind="ExternalInput").ap()
    C_EYE = nc.dram_tensor("C_EYE", [128, 128], F32, kind="ExternalInput").ap()
    C_IOTA = nc.dram_tensor("C_IOTA", [1, SEQ], F32, kind="ExternalInput").ap()
    C_IOTAC = nc.dram_tensor("C_IOTAC", [128, 1], F32, kind="ExternalInput").ap()
    C_MASK = nc.dram_tensor("C_MASK", [128, 128], F32, kind="ExternalInput").ap()
    Y = nc.dram_tensor("Y", [SEQ, D], F32, kind="ExternalOutput").ap()
    aps = {"W_kqv": Wk, "m": m, "C_EYE": C_EYE, "C_IOTA": C_IOTA,
           "C_IOTAC": C_IOTAC, "C_MASK": C_MASK}
    with ExitStack() as ctx:
        p = P(nc, ctx)
        alloc_sb2(p)
        emit_sb2(p, X, Y, aps)
    return nc


def sb2_consts():
    return {
        "C_EYE": np.eye(128, dtype=np.float32),
        "C_IOTA": np.arange(SEQ, dtype=np.float32)[None, :],
        "C_IOTAC": np.arange(128, dtype=np.float32)[:, None],
        "C_MASK": np.where(np.arange(128)[:, None] > np.arange(128)[None, :],
                           np.float32(-1e30), np.float32(0.0)),
    }


def run_sb2_dev(x_seqs, W_kqv, m, core_ids=None):
    n = x_seqs.shape[0]
    if core_ids is None:
        core_ids = list(range(n))
    nc = build_sb2_program()
    consts = sb2_consts()
    in_maps = []
    for i in range(n):
        im = {"X": np.ascontiguousarray(x_seqs[i]),
              "W_kqv": W_kqv, "m": m}
        im.update(consts)
        in_maps.append(im)
    res = run_bass_kernel_spmd(nc, in_maps, core_ids=core_ids)
    return np.stack([r["Y"] for r in res.results])


def phase_sb0_consts(p, aps):
    nc = p.nc
    B = p.buf
    b = p.base("c_dc")
    with nc.Block() as blk:
        @blk.gpsimd
        def _(g):
            p.inc(g.dma_start(out=B["idbf"][:, :], in_=aps["C_EYE"]), "c_dc", 16)
            p.inc(g.dma_start(
                out=B["b_wo"][:, :].rearrange("p (a n) -> p a n", a=8),
                in_=aps["Wo"].rearrange("(a p) n -> p a n", p=128)), "c_dc", 16)

        @blk.vector
        def _(v):
            v.memset(B["idf32"][:, :], 1.0)
            v.memset(B["b_nlam4"][:, :], 1.0)
            v.memset(B["cst"][:, 0:1], -40.0)

        @blk.sync
        def _(s):
            for nm, dst in (("lq1", "b_l0"), ("lk1", "b_l1"),
                            ("lq2", "b_l2"), ("lk2", "b_l3")):
                p.inc(s.dma_start(out=B[dst][:, :],
                                  in_=aps[nm].rearrange("(a d) -> a d", a=1)), "c_dc", 16)
            p.inc(s.dma_start(out=B["b_grep"][:, :],
                              in_=aps["ln_g"].rearrange("(a d) -> a d", a=1).to_broadcast([128, 128])),
                  "c_dc", 16)
            p.inc(s.dma_start(out=B["b_brep"][:, :],
                              in_=aps["ln_b"].rearrange("(a d) -> a d", a=1).to_broadcast([128, 128])),
                  "c_dc", 16)
            s.wait_ge(p.sem["c_dc"], b["c_dc"] + 8 * 16)


def phase_sb0_lam(p, aps, scr_ap):
    nc = p.nc
    B = p.buf
    lam_init = 0.8 - 0.6 * math.exp(-0.3 * 12)
    b = p.base("l_v", "l_s", "l_d")
    with nc.Block() as blk:
        @blk.vector
        def _(v):
            sc = B["b_sc"]
            v.tensor_mul(B["b_l0"][:, :], B["b_l0"][:, :], B["b_l1"][:, :])
            v.tensor_mul(B["b_l2"][:, :], B["b_l2"][:, :], B["b_l3"][:, :])
            v.drain()
            v.reduce_sum(sc[0:1, 0:1], B["b_l0"][:, :], axis=AX.X)
            v.reduce_sum(sc[0:1, 1:2], B["b_l2"][:, :], axis=AX.X)
            p.inc(v.drain(), "l_v")
            v.wait_ge(p.sem["l_s"], b["l_s"] + 1)
            v.tensor_sub(sc[0:1, 4:5], sc[0:1, 2:3], sc[0:1, 3:4])
            v.drain()
            v.tensor_scalar_add(sc[0:1, 4:5], sc[0:1, 4:5], lam_init)
            v.drain()
            v.tensor_scalar_mul(sc[0:1, 5:6], sc[0:1, 4:5], -1.0)
            p.inc(v.drain(), "l_v")

        @blk.scalar
        def _(sc_e):
            sc = B["b_sc"]
            sc_e.wait_ge(p.sem["l_v"], b["l_v"] + 1)
            sc_e.activation(sc[0:1, 2:3], sc[0:1, 0:1], AF.Exp)
            sc_e.activation(sc[0:1, 3:4], sc[0:1, 1:2], AF.Exp)
            p.inc(sc_e.drain(), "l_s")

        @blk.sync
        def _(s):
            s.wait_ge(p.sem["l_v"], b["l_v"] + 2)
            p.inc(s.dma_start(out=scr_ap.rearrange("(a d) -> a d", a=1)[0:1, 0:1], in_=B["b_sc"][0:1, 5:6]), "l_d", 16)
            s.wait_ge(p.sem["l_d"], b["l_d"] + 16)
            p.inc(s.dma_start(out=B["b_nlam"][:, :],
                              in_=scr_ap.rearrange("(a d) -> a d", a=1)[0:1, 0:1].to_broadcast([128, 1])),
                  "l_d", 16)
            s.wait_ge(p.sem["l_d"], b["l_d"] + 32)

    b2 = p.base("l_v2")
    with nc.Block() as blk:
        @blk.vector
        def _(v):
            v.tensor_scalar(B["b_nlam4"][:, :], B["b_nlam4"][:, :],
                            B["b_nlam"][:, 0:1], None, ALU.mult)
            p.inc(v.drain(), "l_v2")


def phase_sb0_proj(p, aps):
    nc = p.nc
    B = p.buf
    xnT, wg = B["xnT"], B["wg"]
    psmm = [B["psA"], B["psB"]]
    vx_v = B["b_vx"][:, :].rearrange("p (t h c) -> p t h c", t=NT, h=8, c=130)

    for W_name, kind, scl in (("Wq", "qT", 0.125), ("Wk", "kT", 1.0),
                              ("Wv", "v", 1.0)):
        Wp = aps[W_name].rearrange("(a p) n -> p a n", p=128)
        dst = B["b_qT"] if kind == "qT" else B["b_kT"]
        for half in range(2):
            b = p.base("j_dw", "j_mm", "j_ev")
            with nc.Block() as blk:
                @blk.gpsimd
                def _(g):
                    src = Wp[:, :, 512 * half:512 * half + 512]
                    p.inc(g.dma_start(out=wg[:, :].rearrange("p (a n) -> p a n", a=8),
                                      in_=src), "j_dw", 16)

                if kind in ("qT", "kT"):
                    @blk.tensor
                    def _(t):
                        t.wait_ge(p.sem["j_dw"], b["j_dw"] + 16)
                        n = 0
                        for cc in range(4):
                            for c in range(4):
                                ps = psmm[n % 2]
                                if n >= 2:
                                    t.wait_ge(p.sem["j_ev"], b["j_ev"] + n - 1)
                                for kd in range(ND):
                                    ins = t.matmul(
                                        ps[:, 0:512],
                                        wg[:, kd * 512 + 128 * cc:kd * 512 + 128 * cc + 128],
                                        xnT[:, kd * SEQ + 512 * c:kd * SEQ + 512 * c + 512],
                                        start=(kd == 0), stop=(kd == ND - 1))
                                p.inc(ins, "j_mm")
                                n += 1

                    @blk.scalar
                    def _(sc):
                        n = 0
                        for cc in range(4):
                            gcc = 4 * half + cc
                            for c in range(4):
                                ps = psmm[n % 2]
                                sc.wait_ge(p.sem["j_mm"], b["j_mm"] + n + 1)
                                p.inc(sc.activation(
                                    dst[:, SEQ * gcc + 512 * c:SEQ * gcc + 512 * c + 512],
                                    ps[:, 0:512], AF.Identity, scale=scl), "j_ev")
                                n += 1
                else:
                    @blk.tensor
                    def _(t):
                        t.wait_ge(p.sem["j_dw"], b["j_dw"] + 16)
                        for tt in range(NT):
                            ps = psmm[tt % 2]
                            if tt >= 2:
                                t.wait_ge(p.sem["j_ev"], b["j_ev"] + tt - 1)
                            for kd in range(ND):
                                ins = t.matmul(
                                    ps[:, 0:512],
                                    xnT[:, kd * SEQ + 128 * tt:kd * SEQ + 128 * tt + 128],
                                    wg[:, kd * 512:kd * 512 + 512],
                                    start=(kd == 0), stop=(kd == ND - 1))
                            p.inc(ins, "j_mm")

                    @blk.scalar
                    def _(sc):
                        for tt in range(NT):
                            ps = psmm[tt % 2]
                            sc.wait_ge(p.sem["j_mm"], b["j_mm"] + tt + 1)
                            srcap = bass.AP(ps[:, :].tensor, ps[:, :].offset,
                                            [ps[:, :].ap[0], [128, 4], [64, 2], [1, 64]])
                            d0 = vx_v[:, tt, 4 * half:4 * half + 4, :]
                            dstap = bass.AP(d0.tensor, d0.offset,
                                            [d0.ap[0], [130, 4], [65, 2], [1, 64]])
                            p.inc(sc.activation(dstap, srcap, AF.Copy), "j_ev")

    b = p.base("j_pr")
    with nc.Block() as blk:
        @blk.vector
        def _(v):
            v.memset(vx_v[:, :, :, 64:65], 1.0)
            v.memset(vx_v[:, :, :, 129:130], 1.0)
            p.inc(v.drain(), "j_pr")


def phase_sb0_attn(p, X_in, X_out, qc):
    nc = p.nc
    B = p.buf
    qT, kT = B["b_qT"], B["b_kT"]
    vx_v = B["b_vx"][:, :].rearrange("p (t h c) -> p t h c", t=NT, h=8, c=130)
    pts = [B["pt0"], B["pt1"]]
    psS = [B["psA"], B["psB"]]
    psCL = B["psC"]
    psCH = B["psTB"][:, 0:512]
    psOT = B["psTB"][:, 512:1024]
    psT = B["psTA"]
    psZc = B["psZ"]
    out_attn = B["out_attn"]
    olnT = B["b_olnT"]
    cbufs = [[B["b_c0lo"], B["b_c0hi"]], [B["b_c1lo"], B["b_c1hi"]]]
    zrows = [B["b_z0"], B["b_z1"]]
    cols = B["cols"]
    H = 8

    b = p.base("b_ts", "b_se", "b_tc", "b_zr", "b_ce", "b_tz", "b_tt", "b_vq",
               "b_sr", "b_vc", "b_ot", "b_oe", "b_wm", "b_we")

    def ctx_pair(t, vh, j, kt):
        g = (2 * vh + j) * NT + kt
        t.wait_ge(p.sem["b_se"], b["b_se"] + g + 1)
        p.inc(t.matmul(psCL[0:65, 0:512], vx_v[:, kt, vh, 0:65],
                       pts[kt % 2][:, :], start=(kt == 0), stop=(kt == NT - 1)),
              "b_tc")
        p.inc(t.matmul(psCH[0:65, 0:512], vx_v[:, kt, vh, 65:130],
                       pts[kt % 2][:, :], start=(kt == 0), stop=(kt == NT - 1)),
              "b_tc")

    def olnT_mms(t, vh):
        t.wait_ge(p.sem["b_vc"], b["b_vc"] + vh + 1)
        if vh >= 1:
            t.wait_ge(p.sem["b_oe"], b["b_oe"] + vh)
        for jq in range(4):
            ins = t.matmul(psOT[:, 128 * jq:128 * jq + 128],
                           B["b_oln"][:, 128 * jq:128 * jq + 128],
                           B["idbf"][:, :], start=True, stop=True)
            if jq == 3:
                p.inc(ins, "b_ot")

    with nc.Block() as blk:
        @blk.tensor
        def _(t):
            for vh in range(H):
                for j in range(2):
                    hh = 2 * vh + j
                    r0 = 64 * (hh % 2)
                    cc = hh // 2
                    nuse = 2 * vh + j
                    if nuse >= 1:
                        t.wait_ge(p.sem["b_ce"], b["b_ce"] + 2 * nuse)
                        t.wait_ge(p.sem["b_zr"], b["b_zr"] + nuse)
                    for kt in range(NT):
                        g = (2 * vh + j) * NT + kt
                        ps = psS[kt % 2]
                        if g >= 2:
                            t.wait_ge(p.sem["b_se"], b["b_se"] + g - 1)
                        p.inc(t.matmul(
                            ps[:, 0:512],
                            kT[r0:r0 + 64, SEQ * cc + 128 * kt:SEQ * cc + 128 * kt + 128],
                            qT[r0:r0 + 64, SEQ * cc + 512 * qc:SEQ * cc + 512 * qc + 512],
                            start=True, stop=True), "b_ts")
                        if kt >= 1:
                            ctx_pair(t, vh, j, kt - 1)
                    ctx_pair(t, vh, j, NT - 1)
                t.wait_ge(p.sem["b_zr"], b["b_zr"] + 2 * vh + 2)
                if vh >= 1:
                    t.wait_ge(p.sem["b_vc"], b["b_vc"] + vh)
                for j in range(2):
                    for jq in range(4):
                        ins = t.matmul(psZc[:, 4 * j + jq:4 * j + jq + 1],
                                       zrows[j][0:1, 128 * jq:128 * jq + 128],
                                       B["idf32"][0:1, 0:1], start=True, stop=True)
                        if j == 1 and jq == 3:
                            p.inc(ins, "b_tz")
                t.wait_ge(p.sem["b_ce"], b["b_ce"] + 4 * (vh + 1))
                n = 0
                for jq in range(4):
                    for m, cb in enumerate((cbufs[0][0], cbufs[0][1],
                                            cbufs[1][0], cbufs[1][1])):
                        ins = t.matmul(psT[:, 256 * jq + 64 * m:256 * jq + 64 * m + 64],
                                       cb[0:64, 128 * jq:128 * jq + 128],
                                       B["idbf"][0:64, 0:64], start=True, stop=True)
                        n += 1
                        if n == 16:
                            p.inc(ins, "b_tt")
                if vh >= 1:
                    olnT_mms(t, vh - 1)
            olnT_mms(t, H - 1)
            t.wait_ge(p.sem["b_oe"], b["b_oe"] + H)
            n = 0
            for jq in range(4):
                for dc in range(2):
                    ps = psS[n % 2]
                    if n >= 2:
                        t.wait_ge(p.sem["b_we"], b["b_we"] + n - 1)
                    for vh in range(H):
                        ins = t.matmul(
                            ps[:, 0:512],
                            olnT[:, 512 * vh + 128 * jq:512 * vh + 128 * jq + 128],
                            B["b_wo"][:, 1024 * vh + 512 * dc:1024 * vh + 512 * dc + 512],
                            start=(vh == 0), stop=(vh == H - 1))
                    p.inc(ins, "b_wm")
                    n += 1

        @blk.scalar
        def _(sc):
            for vh in range(H):
                for j in range(2):
                    for kt in range(NT):
                        g = (2 * vh + j) * NT + kt
                        ps = psS[kt % 2]
                        sc.wait_ge(p.sem["b_ts"], b["b_ts"] + g + 1)
                        if g >= 2:
                            sc.wait_ge(p.sem["b_tc"], b["b_tc"] + 2 * (g - 1))
                        p.inc(sc.activation(pts[kt % 2][:, :], ps[:, 0:512],
                                            AF.Exp, bias=B["cst"][:, 0:1]), "b_se")
                sc.wait_ge(p.sem["b_vq"], b["b_vq"] + vh + 1)
                sc.activation(cols[:, 12:16], cols[:, 8:12], AF.Sqrt)
                p.inc(sc.drain(), "b_sr")

        @blk.vector
        def _(v):
            for vh in range(H):
                for j in range(2):
                    v.wait_ge(p.sem["b_tc"], b["b_tc"] + 2 * NT * (2 * vh + j + 1))
                    v.tensor_copy(zrows[j][0:1, :], psCL[64:65, 0:512])
                    p.inc(v.drain(), "b_zr")
                    if j == 0 and vh >= 1:
                        v.wait_ge(p.sem["b_tt"], b["b_tt"] + vh)
                    v.tensor_copy(cbufs[j][0][0:64, :], psCL[0:64, 0:512])
                    v.tensor_copy(cbufs[j][1][0:64, :], psCH[0:64, 0:512])
                    p.inc(v.drain(), "b_ce", 2)
                if vh >= 1:
                    v.wait_ge(p.sem["b_ot"], b["b_ot"] + vh)
                    v.tensor_copy(olnT[:, 512 * (vh - 1):512 * vh], psOT[:, 0:512])
                    p.inc(v.drain(), "b_oe")
                v.wait_ge(p.sem["b_tz"], b["b_tz"] + vh + 1)
                v.wait_ge(p.sem["b_tt"], b["b_tt"] + vh + 1)
                if vh >= 1:
                    v.wait_ge(p.sem["b_sr"], b["b_sr"] + vh)
                zrs = B["b_zrs"]
                v.reciprocal(zrs[:, :], psZc[:, 0:8])
                v.drain()
                v.tensor_mul(B["b_r1l"][:, :], zrs[:, 4:8], B["b_nlam4"][:, :])
                v.drain()
                c0v = bass.AP(psT[:, :].tensor, psT[:, :].offset,
                              [psT[:, :].ap[0], [256, 4], [1, 128]])
                c1v = bass.AP(psT[:, :].tensor, psT[:, :].offset + 128,
                              [psT[:, :].ap[0], [256, 4], [1, 128]])
                r0bc = bass.AP(zrs[:, :].tensor, zrs[:, :].offset,
                               [zrs[:, :].ap[0], [1, 4], [0, 128]])
                r1bc = bass.AP(B["b_r1l"][:, :].tensor, B["b_r1l"][:, :].offset,
                               [B["b_r1l"][:, :].ap[0], [1, 4], [0, 128]])
                ob = B["b_obuf"][:, :].rearrange("p (a c) -> p a c", a=4)
                ub = B["b_ubuf"][:, :].rearrange("p (a c) -> p a c", a=4)
                v.tensor_tensor(ob, c0v, r0bc, ALU.mult)
                v.tensor_tensor(ub, c1v, r1bc, ALU.mult)
                v.drain()
                v.tensor_add(ob, ob, ub)
                v.drain()
                v.reduce_sum(cols[:, 0:4], ob, axis=AX.X)
                v.tensor_mul(ub, ob, ob)
                v.drain()
                v.reduce_sum(cols[:, 4:8], ub, axis=AX.X)
                v.drain()
                v.tensor_scalar_mul(cols[:, 0:4], cols[:, 0:4], 1.0 / 128.0)
                v.tensor_scalar_mul(cols[:, 4:8], cols[:, 4:8], 1.0 / 128.0)
                v.drain()
                v.tensor_mul(cols[:, 8:12], cols[:, 0:4], cols[:, 0:4])
                v.drain()
                v.tensor_sub(cols[:, 4:8], cols[:, 4:8], cols[:, 8:12])
                v.drain()
                v.tensor_scalar_add(cols[:, 4:8], cols[:, 4:8], 1e-5)
                v.drain()
                v.reciprocal(cols[:, 8:12], cols[:, 4:8])
                p.inc(v.drain(), "b_vq")
                v.wait_ge(p.sem["b_sr"], b["b_sr"] + vh + 1)
                mubc = bass.AP(cols[:, :].tensor, cols[:, :].offset,
                               [cols[:, :].ap[0], [1, 4], [0, 128]])
                rsbc = bass.AP(cols[:, :].tensor, cols[:, :].offset + 12,
                               [cols[:, :].ap[0], [1, 4], [0, 128]])
                v.tensor_tensor(ob, ob, mubc, ALU.subtract)
                v.drain()
                v.tensor_tensor(ob, ob, rsbc, ALU.mult)
                v.drain()
                grepb = bass.AP(B["b_grep"][:, :].tensor, B["b_grep"][:, :].offset,
                                [B["b_grep"][:, :].ap[0], [0, 4], [1, 128]])
                brepb = bass.AP(B["b_brep"][:, :].tensor, B["b_brep"][:, :].offset,
                                [B["b_brep"][:, :].ap[0], [0, 4], [1, 128]])
                v.tensor_tensor(ob, ob, grepb, ALU.mult)
                v.drain()
                olv = B["b_oln"][:, :].rearrange("p (a c) -> p a c", a=4)
                v.tensor_tensor(olv, ob, brepb, ALU.add)
                p.inc(v.drain(), "b_vc")
            v.wait_ge(p.sem["b_ot"], b["b_ot"] + H)
            v.tensor_copy(olnT[:, 512 * (H - 1):512 * H], psOT[:, 0:512])
            p.inc(v.drain(), "b_oe")
            n = 0
            for jq in range(4):
                for dc in range(2):
                    ps = psS[n % 2]
                    v.wait_ge(p.sem["b_wm"], b["b_wm"] + n + 1)
                    v.tensor_copy(
                        out_attn[:, 1024 * jq + 512 * dc:1024 * jq + 512 * dc + 512],
                        ps[:, 0:512])
                    p.inc(v.drain(), "b_we")
                    n += 1

    phase_residual_out(p, X_in, X_out, out_attn, 4 * qc, 4)


def emit_sb0(p, X_in, X_out, aps, scr_ap):
    p.mark("0c")
    phase_sb0_consts(p, aps)
    p.mark("0ln")
    phase_ln_transpose_in(p, X_in)
    p.mark("0lam")
    phase_sb0_lam(p, aps, scr_ap)
    p.mark("0proj")
    phase_sb0_proj(p, aps)
    for qc in range(4):
        p.mark(f"0at{qc}")
        phase_sb0_attn(p, X_in, X_out, qc)


def build_sb0_program():
    nc = bass.Bass("TRN2", target_bir_lowering=False, debug=False)
    X = nc.dram_tensor("X", [SEQ, D], F32, kind="ExternalInput").ap()
    aps = {}
    for nm, shp in (("Wq", [D, D]), ("Wk", [D, D]), ("Wv", [D, D]),
                    ("Wo", [D, D]), ("lq1", [64]), ("lk1", [64]),
                    ("lq2", [64]), ("lk2", [64]), ("ln_g", [128]),
                    ("ln_b", [128])):
        aps[nm] = nc.dram_tensor(nm, shp, F32, kind="ExternalInput").ap()
    aps["C_EYE"] = nc.dram_tensor("C_EYE", [128, 128], F32, kind="ExternalInput").ap()
    aps["C_IOTA"] = nc.dram_tensor("C_IOTA", [1, SEQ], F32, kind="ExternalInput").ap()
    aps["C_IOTAC"] = nc.dram_tensor("C_IOTAC", [128, 1], F32, kind="ExternalInput").ap()
    Y = nc.dram_tensor("Y", [SEQ, D], F32, kind="ExternalOutput").ap()
    scr = nc.dram_tensor("SCR", [64], F32, kind="Internal").ap()
    with ExitStack() as ctx:
        p = P(nc, ctx)
        alloc_common(p)
        alloc_sb0_extra(p)
        emit_sb0(p, X, Y, aps, scr)
    return nc


def run_sb0_dev(x_seqs, W):
    n = x_seqs.shape[0]
    nc = build_sb0_program()
    consts = sb2_consts()
    in_maps = []
    for i in range(n):
        im = {"X": np.ascontiguousarray(x_seqs[i]),
              "Wq": W["L0_attn_Wq"], "Wk": W["L0_attn_Wk"],
              "Wv": W["L0_attn_Wv"], "Wo": W["L0_attn_Wo"],
              "lq1": W["L0_attn_lq1"], "lk1": W["L0_attn_lk1"],
              "lq2": W["L0_attn_lq2"], "lk2": W["L0_attn_lk2"],
              "ln_g": W["L0_attn_ln_g"], "ln_b": W["L0_attn_ln_b"],
              "C_EYE": consts["C_EYE"], "C_IOTA": consts["C_IOTA"],
              "C_IOTAC": consts["C_IOTAC"]}
        in_maps.append(im)
    res = run_bass_kernel_spmd(nc, in_maps, core_ids=list(range(n)))
    return np.stack([r["Y"] for r in res.results])


def alloc_sb3_outer(p):
    p.sb("f_maskd", [128, 128], F32)
    p.sb("f_maskl", [128, 128], F32)
    p.sb("f_gb", [128, 512], BF16)


def alloc_sb3_attn(p):
    p.sb("f_qT", [128, 4 * SEQ], BF16)
    p.sb("f_kT", [128, 4 * SEQ], BF16)
    p.sb("f_v", [128, NT * 512], BF16)
    p.sb("f_oT", [128, 4 * SEQ], BF16)
    p.sb("f_wo", [128, 4 * 1024], BF16)
    p.sb("f_qtm", [128, 2 * 512], F32)
    p.sb("f_ktm", [128, 2 * 512], F32)
    p.sb("f_ub", [128, 512], F32)
    p.sb("f_rr", [128, 512], F32)
    p.sb("f_rcs", [128, 4 * 64], F32)
    p.sb("f_csq", [128, NT * 64], F32)
    p.sb("f_snq", [128, NT * 64], F32)
    p.sb("f_csk", [128, NT * 64], F32)
    p.sb("f_snk", [128, NT * 64], F32)
    p.sb("f_qsr", [128, 64], F32)
    p.sb("f_qsrr", [128, 64], F32)
    p.sb("f_ksr", [128, 64], F32)
    p.sb("f_ksrr", [128, 64], F32)
    p.sb("f_pb", [128, 2 * 640], BF16)
    p.sb("f_ptb", [128, 2 * 640], BF16)


def alloc_sb3_ffnA(p):
    p.sb("f_acc", [128, NT * D], F32)
    p.sb("f_b1c", [128, 44], F32)
    p.sb("f_b2r", [128, D], F32)


def alloc_sb3_hnT(p):
    p.sb("f_lgr", [128, D], F32)
    p.sb("f_lbr", [128, D], F32)


def alloc_sb3_w(p):
    p.sb("f_w1t", [128, 8 * 8 * 256], BF16)
    p.sb("f_w2t", [128, 8 * 1024], BF16)
    p.sb("f_ff", [128, 8 * 512], BF16)


def phase_sb3_consts(p, aps):
    nc = p.nc
    B = p.buf
    b = p.base("c_dc")
    with nc.Block() as blk:
        @blk.gpsimd
        def _(g):
            p.inc(g.dma_start(out=B["idbf"][:, :], in_=aps["C_EYE"]), "c_dc", 16)

        @blk.vector
        def _(v):
            v.memset(B["idf32"][:, :], 1.0)

        @blk.sync
        def _(s):
            p.inc(s.dma_start(out=B["f_maskd"][:, :], in_=aps["C_MASKD"]), "c_dc", 16)
            p.inc(s.dma_start(out=B["f_maskl"][:, :], in_=aps["C_MASKL"]), "c_dc", 16)
            s.wait_ge(p.sem["c_dc"], b["c_dc"] + 3 * 16)


def phase_sb3_attn_consts(p, aps, l):
    nc = p.nc
    B = p.buf
    b = p.base("c_dl")
    with nc.Block() as blk:
        @blk.gpsimd
        def _(g):
            p.inc(g.dma_start(
                out=B["f_wo"][:, :].rearrange("p (a n) -> p a n", a=4),
                in_=aps["wo"][l].rearrange("(a p) n -> p a n", p=128)), "c_dl", 16)

        @blk.sync
        def _(s):
            for nm, dst in (("C_CSQ", "f_csq"), ("C_SNQ", "f_snq"),
                            ("C_CSK", "f_csk"), ("C_SNK", "f_snk")):
                p.inc(s.dma_start(
                    out=B[dst][:, :].rearrange("p (t c) -> p t c", t=NT),
                    in_=aps[nm].rearrange("(t p) c -> p t c", p=128)), "c_dl", 16)
            qs, ks = aps["q_scale"], aps["k_scale"]
            p.inc(s.dma_start(out=B["f_qsr"][:, :],
                              in_=qs[l:l + 1, :].to_broadcast([128, 64])), "c_dl", 16)
            p.inc(s.dma_start(out=B["f_ksr"][:, :],
                              in_=ks[l:l + 1, :].to_broadcast([128, 64])), "c_dl", 16)
            p.inc(s.dma_start(out=B["f_qsrr"][:, 0:32],
                              in_=qs[l:l + 1, 32:64].to_broadcast([128, 32])), "c_dl", 16)
            p.inc(s.dma_start(out=B["f_qsrr"][:, 32:64],
                              in_=qs[l:l + 1, 0:32].to_broadcast([128, 32])), "c_dl", 16)
            p.inc(s.dma_start(out=B["f_ksrr"][:, 0:32],
                              in_=ks[l:l + 1, 32:64].to_broadcast([128, 32])), "c_dl", 16)
            p.inc(s.dma_start(out=B["f_ksrr"][:, 32:64],
                              in_=ks[l:l + 1, 0:32].to_broadcast([128, 32])), "c_dl", 16)
            s.wait_ge(p.sem["c_dl"], b["c_dl"] + 11 * 16)

        @blk.vector
        def _(v):
            v.wait_ge(p.sem["c_dl"], p.cnt["c_dl"])
            v.tensor_scalar_mul(B["f_qsr"][:, :], B["f_qsr"][:, :], 8.0)
            v.tensor_scalar_mul(B["f_qsrr"][:, :], B["f_qsrr"][:, :], 8.0)
            v.drain()


def phase_sb3_ffn_consts(p, aps, l):
    nc = p.nc
    B = p.buf
    b = p.base("c_df")
    with nc.Block() as blk:
        @blk.sync
        def _(s):
            p.inc(s.dma_start(out=B["f_b2r"][:, :],
                              in_=aps["b2"][l:l + 1, :].to_broadcast([128, D])), "c_df", 16)
            p.inc(s.dma_start(out=B["f_b1c"][:, :], in_=aps["b1p"][l]), "c_df", 16)
            s.wait_ge(p.sem["c_df"], b["c_df"] + 2 * 16)


def phase_sb3_hnT_consts(p, aps, l):
    nc = p.nc
    B = p.buf
    b = p.base("c_dh")
    with nc.Block() as blk:
        @blk.sync
        def _(s):
            p.inc(s.dma_start(out=B["f_lgr"][:, :],
                              in_=aps["ln_g"][l:l + 1, :].to_broadcast([128, D])), "c_dh", 16)
            p.inc(s.dma_start(out=B["f_lbr"][:, :],
                              in_=aps["ln_b"][l:l + 1, :].to_broadcast([128, D])), "c_dh", 16)
            s.wait_ge(p.sem["c_dh"], b["c_dh"] + 2 * 16)


def phase_sb3_shiftT(p, Xp_ap):
    nc = p.nc
    B = p.buf
    hT, xt, xb = B["xnT"], B["xt2"], B["xn2"]
    pst = [B["psTA"], B["psTB"]]
    b = p.base("s_dx", "s_cb", "s_tt", "s_ev")
    hT_v = hT[:, :].rearrange("p (j s) -> p j s", j=ND, s=SEQ)

    def evac(v, i):
        v.wait_ge(p.sem["s_tt"], b["s_tt"] + i + 1)
        ps = pst[i % 2]
        src = ps[:, 0:ND * 128].rearrange("p (j c) -> p j c", j=ND, c=128)
        p.inc(v.tensor_copy(hT_v[:, 0:4, 128 * i:128 * i + 128], src[:, 0:4, :]),
              "s_ev")
        w = 128 if i < NT - 1 else 127
        p.inc(v.tensor_copy(hT_v[:, 4:8, 128 * i + 1:128 * i + 1 + w],
                            src[:, 4:8, 0:w]), "s_ev")

    with nc.Block() as blk:
        @blk.sync
        def _(s):
            for i in range(NT):
                if i >= 2:
                    s.wait_ge(p.sem["s_cb"], b["s_cb"] + i - 1)
                p.inc(s.dma_start(out=xt[:, (i % 2) * D:(i % 2) * D + D],
                                  in_=Xp_ap[1 + 128 * i:1 + 128 * i + 128, :]),
                      "s_dx", 16)

        @blk.scalar
        def _(sc):
            for i in range(NT):
                sc.wait_ge(p.sem["s_dx"], b["s_dx"] + 16 * (i + 1))
                if i >= 2:
                    sc.wait_ge(p.sem["s_tt"], b["s_tt"] + i - 1)
                p.inc(sc.activation(xb[:, 0:D] if False else B["xn2"][:, (i % 2) * D:(i % 2) * D + D],
                                    xt[:, (i % 2) * D:(i % 2) * D + D], AF.Copy),
                      "s_cb")

        @blk.tensor
        def _(t):
            for i in range(NT):
                t.wait_ge(p.sem["s_cb"], b["s_cb"] + i + 1)
                if i >= 2:
                    t.wait_ge(p.sem["s_ev"], b["s_ev"] + 2 * (i - 1))
                ps = pst[i % 2]
                for j in range(ND):
                    ins = t.matmul(ps[:, 128 * j:128 * j + 128],
                                   B["xn2"][:, (i % 2) * D + 128 * j:
                                            (i % 2) * D + 128 * j + 128],
                                   B["idbf"][:, :], start=True, stop=True)
                    if j == ND - 1:
                        p.inc(ins, "s_tt")

        @blk.vector
        def _(v):
            v.memset(hT_v[:, 4:8, 0:1], 0.0)
            for i in range(NT):
                evac(v, i)


def phase_sb3_qkv(p, aps, l):
    nc = p.nc
    B = p.buf
    hT, wg = B["xnT"], B["wg"]
    psmm = [B["psA"], B["psB"]]
    Wp = aps["wqkv"][l].rearrange("(a p) n -> p a n", p=128)

    b = p.base("y_dw", "y_mm", "y_ev")
    with nc.Block() as blk:
        @blk.gpsimd
        def _(g):
            p.inc(g.dma_start(out=wg[:, :].rearrange("p (a n) -> p a n", a=8),
                              in_=Wp[:, :, 1024:1536]), "y_dw", 16)

        @blk.tensor
        def _(t):
            t.wait_ge(p.sem["y_dw"], b["y_dw"] + 16)
            for tt in range(NT):
                ps = psmm[tt % 2]
                if tt >= 2:
                    t.wait_ge(p.sem["y_ev"], b["y_ev"] + tt - 1)
                for kd in range(ND):
                    ins = t.matmul(ps[:, 0:512],
                                   hT[:, kd * SEQ + 128 * tt:kd * SEQ + 128 * tt + 128],
                                   wg[:, kd * 512:kd * 512 + 512],
                                   start=(kd == 0), stop=(kd == ND - 1))
                p.inc(ins, "y_mm")

        @blk.scalar
        def _(sc):
            for tt in range(NT):
                ps = psmm[tt % 2]
                sc.wait_ge(p.sem["y_mm"], b["y_mm"] + tt + 1)
                p.inc(sc.activation(B["f_v"][:, 512 * tt:512 * tt + 512],
                                    ps[:, 0:512], AF.Copy), "y_ev")

    for grp, gcol in (("q", 0), ("k", 512)):
        dst = B["f_qT"] if grp == "q" else B["f_kT"]
        tm = B["f_qtm"] if grp == "q" else B["f_ktm"]
        cs = B["f_csq"] if grp == "q" else B["f_csk"]
        sn = B["f_snq"] if grp == "q" else B["f_snk"]
        sr = B["f_qsr"] if grp == "q" else B["f_ksr"]
        srr = B["f_qsrr"] if grp == "q" else B["f_ksrr"]
        b = p.base("y_dw", "y_mm", "y_vn", "y_sq", "y_pr", "y_vr", "y_tt", "y_ev")
        stg = B["f_gb"]
        with nc.Block() as blk:
            @blk.gpsimd
            def _(g):
                p.inc(g.dma_start(out=wg[:, :].rearrange("p (a n) -> p a n", a=8),
                                  in_=Wp[:, :, gcol:gcol + 512]), "y_dw", 16)

            @blk.tensor
            def _(t):
                t.wait_ge(p.sem["y_dw"], b["y_dw"] + 16)

                def trans(tt):
                    t.wait_ge(p.sem["y_vr"], b["y_vr"] + tt + 1)
                    if tt >= 1:
                        t.wait_ge(p.sem["y_ev"], b["y_ev"] + tt)
                    for pc in range(4):
                        ins = t.matmul(B["psC"][:, 128 * pc:128 * pc + 128],
                                       stg[:, 128 * pc:128 * pc + 128],
                                       B["idbf"][:, :], start=True, stop=True)
                        if pc == 3:
                            p.inc(ins, "y_tt")

                for tt in range(NT):
                    ps = psmm[tt % 2]
                    if tt >= 2:
                        t.wait_ge(p.sem["y_pr"], b["y_pr"] + tt - 1)
                    for kd in range(ND):
                        ins = t.matmul(ps[:, 0:512],
                                       hT[:, kd * SEQ + 128 * tt:kd * SEQ + 128 * tt + 128],
                                       wg[:, kd * 512:kd * 512 + 512],
                                       start=(kd == 0), stop=(kd == ND - 1))
                    p.inc(ins, "y_mm")
                    if tt >= 1:
                        trans(tt - 1)
                trans(NT - 1)

            @blk.vector
            def _(v):
                cols = B["cols"]
                for tt in range(NT):
                    sl = (tt % 2) * 512
                    v.wait_ge(p.sem["y_mm"], b["y_mm"] + tt + 1)
                    if tt >= 1:
                        v.wait_ge(p.sem["y_sq"], b["y_sq"] + tt)
                    ps = psmm[tt % 2]
                    tmv = tm[:, sl:sl + 512].rearrange("p (h c) -> p h c", h=8)
                    v.tensor_copy(tm[:, sl:sl + 512], ps[:, 0:512])
                    p.inc(v.drain(), "y_pr")
                    ub = B["f_ub"][:, :].rearrange("p (h c) -> p h c", h=8)
                    v.tensor_tensor(ub, tmv, tmv, ALU.mult)
                    v.drain()
                    v.reduce_sum(cols[:, 0:8], ub, axis=AX.X)
                    p.inc(v.drain(), "y_vn")
                    v.wait_ge(p.sem["y_sq"], b["y_sq"] + tt + 1)
                    v.tensor_scalar(cols[:, 8:16], cols[:, 8:16], 1e-12, None,
                                    ALU.max)
                    v.drain()
                    v.reciprocal(cols[:, 8:16], cols[:, 8:16])
                    v.drain()
                    rnbc = bass.AP(cols[:, :].tensor, cols[:, :].offset + 8,
                                   [cols[:, :].ap[0], [1, 8], [0, 64]])
                    v.tensor_tensor(tmv, tmv, rnbc, ALU.mult)
                    v.drain()
                    A = B["f_rcs"][:, 0:64]
                    Bm = B["f_rcs"][:, 64:128]
                    v.tensor_mul(A, cs[:, 64 * tt:64 * tt + 64], sr[:, :])
                    v.tensor_mul(Bm, sn[:, 64 * tt:64 * tt + 64], srr[:, :])
                    v.drain()
                    rr = B["f_rr"][:, :].rearrange("p (h c) -> p h c", h=8)
                    tm_lo = bass.AP(tmv.tensor, tmv.offset,
                                    [tmv.ap[0], [64, 8], [1, 32]])
                    tm_hi = bass.AP(tmv.tensor, tmv.offset + 32,
                                    [tmv.ap[0], [64, 8], [1, 32]])
                    rr_lo = bass.AP(rr.tensor, rr.offset,
                                    [rr.ap[0], [64, 8], [1, 32]])
                    rr_hi = bass.AP(rr.tensor, rr.offset + 32,
                                    [rr.ap[0], [64, 8], [1, 32]])
                    Blo = bass.AP(Bm.tensor, Bm.offset, [Bm.ap[0], [0, 8], [1, 32]])
                    Bhi = bass.AP(Bm.tensor, Bm.offset + 32,
                                  [Bm.ap[0], [0, 8], [1, 32]])
                    v.tensor_tensor(rr_lo, tm_hi, Blo, ALU.mult)
                    v.tensor_tensor(rr_hi, tm_lo, Bhi, ALU.mult)
                    Abc = bass.AP(A.tensor, A.offset, [A.ap[0], [0, 8], [1, 64]])
                    v.tensor_tensor(ub, tmv, Abc, ALU.mult)
                    v.drain()
                    v.wait_ge(p.sem["y_tt"], b["y_tt"] + tt)
                    ub_lo = bass.AP(ub.tensor, ub.offset, [ub.ap[0], [64, 8], [1, 32]])
                    ub_hi = bass.AP(ub.tensor, ub.offset + 32,
                                    [ub.ap[0], [64, 8], [1, 32]])
                    sg = stg[:, :].rearrange("p (h c) -> p h c", h=8)
                    sg_lo = bass.AP(sg.tensor, sg.offset, [sg.ap[0], [64, 8], [1, 32]])
                    sg_hi = bass.AP(sg.tensor, sg.offset + 32,
                                    [sg.ap[0], [64, 8], [1, 32]])
                    v.tensor_sub(sg_lo, ub_lo, rr_lo)
                    v.tensor_add(sg_hi, ub_hi, rr_hi)
                    p.inc(v.drain(), "y_vr")

            @blk.scalar
            def _(sc):
                cols = B["cols"]
                for tt in range(NT):
                    sc.wait_ge(p.sem["y_vn"], b["y_vn"] + tt + 1)
                    sc.activation(cols[:, 8:16], cols[:, 0:8], AF.Sqrt)
                    p.inc(sc.drain(), "y_sq")
                    if tt >= 1:
                        sc.wait_ge(p.sem["y_tt"], b["y_tt"] + tt)
                        dsts = bass.AP(dst[:, :].tensor,
                                       dst[:, :].offset + 128 * (tt - 1),
                                       [dst[:, :].ap[0], [SEQ, 4], [1, 128]])
                        p.inc(sc.activation(dsts, B["psC"][:, 0:512], AF.Copy),
                              "y_ev")
                sc.wait_ge(p.sem["y_tt"], b["y_tt"] + NT)
                dsts = bass.AP(dst[:, :].tensor, dst[:, :].offset + 128 * (NT - 1),
                               [dst[:, :].ap[0], [SEQ, 4], [1, 128]])
                p.inc(sc.activation(dsts, B["psC"][:, 0:512], AF.Copy), "y_ev")


def phase_sb3_attn(p, Xp_in, Xp_mid):
    nc = p.nc
    B = p.buf
    qT, kT, vv, oT = B["f_qT"], B["f_kT"], B["f_v"], B["f_oT"]
    psS = [B["psTA"], B["psTB"]]
    psCx = [B["psA"], B["psB"]]
    pbs, ptbs = B["f_pb"], B["f_ptb"]
    cols = B["cols"]
    b = p.base("f_ts", "f_vm", "f_se", "f_vp", "f_tt", "f_pe", "f_tc", "f_oe")

    groups = [(h, i) for h in range(8) for i in range(NT)]

    def nk_of(i):
        return min(i, 4) + 1

    with nc.Block() as blk:
        @blk.tensor
        def _(t):
            def mm_s(n):
                h, i = groups[n]
                nk = nk_of(i)
                k0 = 128 * max(0, i - 4)
                r0 = 64 * (h % 2)
                pc = h // 2
                ps = psS[n % 2]
                if n >= 2:
                    t.wait_ge(p.sem["f_se"], b["f_se"] + n - 1)
                w = 128 * nk
                ins = t.matmul(ps[:, 0:min(w, 512)],
                               qT[r0:r0 + 64, SEQ * pc + 128 * i:SEQ * pc + 128 * i + 128],
                               kT[r0:r0 + 64, SEQ * pc + k0:SEQ * pc + k0 + min(w, 512)],
                               start=True, stop=True)
                if w > 512:
                    ins = t.matmul(ps[:, 512:w],
                                   qT[r0:r0 + 64, SEQ * pc + 128 * i:SEQ * pc + 128 * i + 128],
                                   kT[r0:r0 + 64, SEQ * pc + k0 + 512:SEQ * pc + k0 + w],
                                   start=True, stop=True)
                p.inc(ins, "f_ts")

            def trans(n):
                h, i = groups[n]
                nk = nk_of(i)
                t.wait_ge(p.sem["f_vp"], b["f_vp"] + n + 1)
                if n >= 1:
                    t.wait_ge(p.sem["f_pe"], b["f_pe"] + n)
                pb = pbs[:, (n % 2) * 640:(n % 2) * 640 + 640]
                for kk in range(nk):
                    dstp = (B["psC"][:, 128 * kk:128 * kk + 128] if kk < 4
                            else B["psZ"][:, 0:128])
                    ins = t.matmul(dstp, pb[:, 128 * kk:128 * kk + 128],
                                   B["idbf"][:, :], start=True, stop=True)
                    if kk == nk - 1:
                        p.inc(ins, "f_tt")

            def ctx(n):
                h, i = groups[n]
                nk = nk_of(i)
                k0 = max(0, i - 4)
                r0 = 64 * (h % 2)
                t.wait_ge(p.sem["f_pe"], b["f_pe"] + n + 1)
                if n >= 2:
                    t.wait_ge(p.sem["f_oe"], b["f_oe"] + n - 1)
                ptb = ptbs[:, (n % 2) * 640:(n % 2) * 640 + 640]
                psc = psCx[n % 2]
                for kk in range(nk):
                    ins = t.matmul(psc[r0:r0 + 64, 0:128],
                                   vv[:, 512 * (k0 + kk) + 64 * h:512 * (k0 + kk) + 64 * h + 64],
                                   ptb[:, 128 * kk:128 * kk + 128],
                                   start=(kk == 0), stop=(kk == nk - 1))
                p.inc(ins, "f_tc")

            N = len(groups)
            for n in range(N + 2):
                if n < N:
                    mm_s(n)
                if 0 <= n - 1 < N:
                    trans(n - 1)
                if 0 <= n - 2 < N:
                    ctx(n - 2)

        @blk.vector
        def _(v):
            for n, (h, i) in enumerate(groups):
                nk = nk_of(i)
                ps = psS[n % 2]
                v.wait_ge(p.sem["f_ts"], b["f_ts"] + n + 1)
                if n >= 1:
                    v.wait_ge(p.sem["f_se"], b["f_se"] + n)
                w = 128 * nk
                dcol = 128 * (nk - 1)
                v.tensor_add(ps[:, dcol:dcol + 128], ps[:, dcol:dcol + 128],
                             B["f_maskd"][:, :])
                if i >= 4:
                    v.tensor_add(ps[:, 0:128], ps[:, 0:128], B["f_maskl"][:, :])
                v.drain()
                v.reduce_max(cols[:, n % 2:n % 2 + 1], ps[:, 0:w], axis=AX.X,
                             negate=True)
                p.inc(v.drain(), "f_vm")
                v.wait_ge(p.sem["f_se"], b["f_se"] + n + 1)
                v.reciprocal(cols[:, 4 + n % 2:5 + n % 2],
                             cols[:, 2 + n % 2:3 + n % 2])
                v.drain()
                pb = pbs[:, (n % 2) * 640:(n % 2) * 640 + 640]
                v.tensor_scalar(pb[:, 0:w], pb[:, 0:w],
                                cols[:, 4 + n % 2:5 + n % 2], None, ALU.mult)
                p.inc(v.drain(), "f_vp")

        @blk.scalar
        def _(sc):
            for n, (h, i) in enumerate(groups):
                nk = nk_of(i)
                w = 128 * nk
                ps = psS[n % 2]
                sc.wait_ge(p.sem["f_vm"], b["f_vm"] + n + 1)
                if n >= 2:
                    sc.wait_ge(p.sem["f_vp"], b["f_vp"] + n - 1)
                    sc.wait_ge(p.sem["f_tt"], b["f_tt"] + n - 1)
                pb = pbs[:, (n % 2) * 640:(n % 2) * 640 + 640]
                p.inc(sc.activation(pb[:, 0:w], ps[:, 0:w], AF.Exp,
                                    bias=cols[:, n % 2:n % 2 + 1],
                                    accum_out=cols[:, 2 + n % 2:3 + n % 2]),
                      "f_se")
                if n >= 1:
                    _sb3_pev(p, sc, b, groups, n - 1, ptbs)
                if n >= 2:
                    _sb3_oev(p, sc, b, groups, n - 2, psCx, oT)
            _sb3_pev(p, sc, b, groups, len(groups) - 1, ptbs)
            _sb3_oev(p, sc, b, groups, len(groups) - 2, psCx, oT)
            _sb3_oev(p, sc, b, groups, len(groups) - 1, psCx, oT)

    b2 = p.base("w_dx", "w_mm", "w_sv", "w_va", "w_st")
    with nc.Block() as blk:
        @blk.sync
        def _(s):
            for i in range(NT):
                if i >= 2:
                    s.wait_ge(p.sem["w_st"], b2["w_st"] + 16 * (i - 1))
                p.inc(s.dma_start(out=B["xt2"][:, (i % 2) * D:(i % 2) * D + D],
                                  in_=Xp_in[1 + 128 * i:129 + 128 * i, :]),
                      "w_dx", 16)
                if i >= 1:
                    s.wait_ge(p.sem["w_va"], b2["w_va"] + i)
                    p.inc(s.dma_start(
                        out=Xp_mid[1 + 128 * (i - 1):129 + 128 * (i - 1), :],
                        in_=B["xt2"][:, ((i - 1) % 2) * D:((i - 1) % 2) * D + D]),
                        "w_st", 16)
            s.wait_ge(p.sem["w_va"], b2["w_va"] + NT)
            p.inc(s.dma_start(out=Xp_mid[1 + 128 * (NT - 1):129 + 128 * (NT - 1), :],
                              in_=B["xt2"][:, ((NT - 1) % 2) * D:((NT - 1) % 2) * D + D]),
                  "w_st", 16)
            s.wait_ge(p.sem["w_st"], b2["w_st"] + 16 * NT)

        @blk.tensor
        def _(t):
            for i in range(NT):
                for dc in range(2):
                    n = 2 * i + dc
                    ps = [B["psTA"], B["psTB"]][n % 2]
                    if n >= 2:
                        t.wait_ge(p.sem["w_sv"], b2["w_sv"] + n - 1)
                    for pc in range(4):
                        ins = t.matmul(
                            ps[:, 0:512],
                            oT[:, SEQ * pc + 128 * i:SEQ * pc + 128 * i + 128],
                            B["f_wo"][:, 1024 * pc + 512 * dc:1024 * pc + 512 * dc + 512],
                            start=(pc == 0), stop=(pc == 3))
                    p.inc(ins, "w_mm")

        @blk.scalar
        def _(sc):
            for i in range(NT):
                for dc in range(2):
                    n = 2 * i + dc
                    ps = [B["psTA"], B["psTB"]][n % 2]
                    sc.wait_ge(p.sem["w_mm"], b2["w_mm"] + n + 1)
                    if i >= 1 and dc == 0:
                        sc.wait_ge(p.sem["w_va"], b2["w_va"] + i)
                    p.inc(sc.activation(
                        B["tmp_f32"][:, 512 * dc:512 * dc + 512],
                        ps[:, 0:512], AF.Copy), "w_sv")

        @blk.vector
        def _(v):
            for i in range(NT):
                v.wait_ge(p.sem["w_sv"], b2["w_sv"] + 2 * i + 2)
                v.wait_ge(p.sem["w_dx"], b2["w_dx"] + 16 * (i + 1))
                xo = B["xt2"][:, (i % 2) * D:(i % 2) * D + D]
                v.tensor_add(xo, xo, B["tmp_f32"][:, :])
                p.inc(v.drain(), "w_va")


def _sb3_pev(p, sc, b, groups, n, ptbs):
    h, i = groups[n]
    nk = min(i, 4) + 1
    sc.wait_ge(p.sem["f_tt"], b["f_tt"] + n + 1)
    if n >= 2:
        sc.wait_ge(p.sem["f_tc"], b["f_tc"] + n - 1)
    ptb = ptbs[:, (n % 2) * 640:(n % 2) * 640 + 640]
    w1 = min(nk, 4) * 128
    ins = sc.activation(ptb[:, 0:w1], p.buf["psC"][:, 0:w1], AF.Copy)
    if nk == 5:
        ins = sc.activation(ptb[:, 512:640], p.buf["psZ"][:, 0:128], AF.Copy)
    p.inc(ins, "f_pe")


def _sb3_oev(p, sc, b, groups, n, psCx, oT):
    h, i = groups[n]
    r0 = 64 * (h % 2)
    pc = h // 2
    sc.wait_ge(p.sem["f_tc"], b["f_tc"] + n + 1)
    p.inc(sc.activation(oT[r0:r0 + 64, SEQ * pc + 128 * i:SEQ * pc + 128 * i + 128],
                        psCx[n % 2][r0:r0 + 64, 0:128], AF.Copy), "f_oe")


def phase_sb3_hnT(p, Xp_mid, eps=1e-5, shift=True, gkey="f_lgr", bkey="f_lbr"):
    nc = p.nc
    B = p.buf
    hnT, xt, xn, junk, cols = B["xnT"], B["xt2"], B["xn2"], B["junk"], B["cols"]
    pst = [B["psTA"], B["psTB"]]
    b = p.base("n_dx", "n_vs", "n_sq", "n_nm", "n_vg", "n_tt", "n_ev")
    hnT_v = hnT[:, :].rearrange("p (j s) -> p j s", j=ND, s=SEQ)

    def evac(v, i):
        v.wait_ge(p.sem["n_tt"], b["n_tt"] + i + 1)
        ps = pst[i % 2]
        src = ps[:, 0:ND * 128].rearrange("p (j c) -> p j c", j=ND, c=128)
        p.inc(v.tensor_copy(hnT_v[:, :, 128 * i:128 * i + 128], src), "n_ev")

    with nc.Block() as blk:
        @blk.sync
        def _(s):
            for i in range(NT):
                if i >= 2:
                    s.wait_ge(p.sem["n_nm"], b["n_nm"] + i - 1)
                if shift:
                    p.inc(s.dma_start(out=xt[:, (i % 2) * D:(i % 2) * D + 512],
                                      in_=Xp_mid[1 + 128 * i:129 + 128 * i, 0:512]),
                          "n_dx", 16)
                    p.inc(s.dma_start(out=xt[:, (i % 2) * D + 512:(i % 2) * D + D],
                                      in_=Xp_mid[128 * i:128 * i + 128, 512:1024]),
                          "n_dx", 16)
                else:
                    p.inc(s.dma_start(out=xt[:, (i % 2) * D:(i % 2) * D + 512],
                                      in_=Xp_mid[1 + 128 * i:129 + 128 * i, 0:512]),
                          "n_dx", 16)
                    p.inc(s.dma_start(out=xt[:, (i % 2) * D + 512:(i % 2) * D + D],
                                      in_=Xp_mid[1 + 128 * i:129 + 128 * i, 512:1024]),
                          "n_dx", 16)

        @blk.vector
        def _(v):
            for i in range(NT):
                if i >= 1:
                    v.wait_ge(p.sem["n_nm"], b["n_nm"] + i)
                v.wait_ge(p.sem["n_dx"], b["n_dx"] + 32 * (i + 1))
                xti = xt[:, (i % 2) * D:(i % 2) * D + D]
                v.reduce_sum(cols[:, 0:1], xti, axis=AX.X)
                v.wait_ge(p.sem["n_sq"], b["n_sq"] + i + 1)
                v.drain()
                v.tensor_scalar_mul(cols[:, 2:3], cols[:, 0:1], 1.0 / D)
                v.drain()
                v.tensor_mul(cols[:, 3:4], cols[:, 2:3], cols[:, 2:3])
                v.tensor_scalar(cols[:, 4:5], cols[:, 1:2], 1.0 / D, None, ALU.mult)
                v.drain()
                v.tensor_sub(cols[:, 5:6], cols[:, 4:5], cols[:, 3:4])
                v.drain()
                v.tensor_scalar_add(cols[:, 5:6], cols[:, 5:6], eps)
                v.drain()
                v.reciprocal(cols[:, 6:7], cols[:, 5:6])
                p.inc(v.drain(), "n_vs")
                v.wait_ge(p.sem["n_nm"], b["n_nm"] + i + 1)
                v.tensor_mul(B["tmp_f32"][:, :], B["tmp_f32"][:, :], B[gkey][:, :])
                v.drain()
                xni = xn[:, (i % 2) * D:(i % 2) * D + D]
                if i >= 2:
                    v.wait_ge(p.sem["n_tt"], b["n_tt"] + i - 1)
                v.tensor_add(xni, B["tmp_f32"][:, :], B[bkey][:, :])
                p.inc(v.drain(), "n_vg")
                if i >= 1:
                    evac(v, i - 1)
            evac(v, NT - 1)

        @blk.scalar
        def _(sc):
            for i in range(NT):
                sc.wait_ge(p.sem["n_dx"], b["n_dx"] + 32 * (i + 1))
                if i >= 1:
                    sc.wait_ge(p.sem["n_vs"], b["n_vs"] + i)
                xti = xt[:, (i % 2) * D:(i % 2) * D + D]
                sc.activation(junk[:, :], xti, AF.Square, accum_out=cols[:, 1:2])
                p.inc(sc.drain(), "n_sq")
                sc.wait_ge(p.sem["n_vs"], b["n_vs"] + i + 1)
                sc.activation(cols[:, 7:8], cols[:, 6:7], AF.Sqrt)
                sc.mul(cols[:, 8:9], cols[:, 2:3], -1.0)
                sc.drain()
                sc.activation(cols[:, 9:10], cols[:, 8:9], AF.Identity,
                              scale=cols[:, 7:8])
                sc.drain()
                if i >= 1:
                    sc.wait_ge(p.sem["n_vg"], b["n_vg"] + i)
                p.inc(sc.activation(B["tmp_f32"][:, :], xti, AF.Identity,
                                    scale=cols[:, 7:8], bias=cols[:, 9:10]),
                      "n_nm")

        @blk.tensor
        def _(t):
            for i in range(NT):
                t.wait_ge(p.sem["n_vg"], b["n_vg"] + i + 1)
                if i >= 2:
                    t.wait_ge(p.sem["n_ev"], b["n_ev"] + i - 1)
                ps = pst[i % 2]
                for j in range(ND):
                    ins = t.matmul(ps[:, 128 * j:128 * j + 128],
                                   xn[:, (i % 2) * D + 128 * j:
                                       (i % 2) * D + 128 * j + 128],
                                   B["idbf"][:, :], start=True, stop=True)
                    if j == ND - 1:
                        p.inc(ins, "n_tt")


def phase_sb3_ffn(p, aps, l, Xp_mid, Xp_out):
    nc = p.nc
    B = p.buf
    hnT, acc = B["xnT"], B["f_acc"]
    w1t, w2t, ffb = B["f_w1t"], B["f_w2t"], B["f_ff"]
    psmm = [B["psA"], B["psB"]]
    psO = [B["psTA"], B["psTB"]]
    NTL = 22
    tgs = [list(range(0, 8)), list(range(8, 16)), list(range(16, 22))]
    W1 = aps["w1p"][l].rearrange("(a p) n -> p a n", p=128)
    W2 = aps["w2p"][l].rearrange("(t p) n -> p t n", p=128)

    bz = p.base("g_z")
    with nc.Block() as blk:
        @blk.vector
        def _(v):
            v.memset(acc[:, :], 0.0)
            p.inc(v.drain(), "g_z")

    for gi, tg in enumerate(tgs):
        b = p.base("g_dw", "g_am", "g_gm", "g_ge", "g_ff", "g_om", "g_ac")
        ng = len(tg)
        with nc.Block() as blk:
            @blk.gpsimd
            def _(g):
                for x, tt in enumerate(tg):
                    p.inc(g.dma_start(
                        out=w1t[:, :].rearrange("p (x a n) -> p x a n", x=8, a=8)[:, x],
                        in_=W1[:, :, 256 * tt:256 * tt + 256]), "g_dw", 16)
                    p.inc(g.dma_start(
                        out=w2t[:, 1024 * x:1024 * x + 1024],
                        in_=W2[:, tt, :]), "g_dw", 16)

            @blk.tensor
            def _(t):
                t.wait_ge(p.sem["g_dw"], b["g_dw"] + 32 * ng)
                for ck in range(4):
                    for x in range(ng):
                        n = ck * ng + x
                        psa, psg = psmm[0], psmm[1]
                        if n >= 1:
                            t.wait_ge(p.sem["g_ge"], b["g_ge"] + n)
                            t.wait_ge(p.sem["g_ff"], b["g_ff"] + n)
                        for kd in range(ND):
                            t.matmul(psa[:, 0:512],
                                     w1t[:, 2048 * x + 256 * kd:2048 * x + 256 * kd + 128],
                                     hnT[:, kd * SEQ + 512 * ck:kd * SEQ + 512 * ck + 512],
                                     start=(kd == 0), stop=(kd == ND - 1))
                        for kd in range(ND):
                            ins = t.matmul(psg[:, 0:512],
                                           w1t[:, 2048 * x + 256 * kd + 128:2048 * x + 256 * kd + 256],
                                           hnT[:, kd * SEQ + 512 * ck:kd * SEQ + 512 * ck + 512],
                                           start=(kd == 0), stop=(kd == ND - 1))
                        p.inc(ins, "g_am")
                    for ts4 in range(4):
                        for dc in range(2):
                            m = ck * 8 + ts4 * 2 + dc
                            ps = psO[m % 2]
                            if m >= 2:
                                t.wait_ge(p.sem["g_ac"], b["g_ac"] + m - 1)
                            for x in range(ng):
                                t.wait_ge(p.sem["g_ff"], b["g_ff"] + ck * ng + x + 1)
                                ins = t.matmul(
                                    ps[:, 0:512],
                                    ffb[:, 512 * x + 128 * ts4:512 * x + 128 * ts4 + 128],
                                    w2t[:, 1024 * x + 512 * dc:1024 * x + 512 * dc + 512],
                                    start=(x == 0), stop=(x == ng - 1))
                            p.inc(ins, "g_om")

            @blk.scalar
            def _(sc):
                for ck in range(4):
                    for x in range(ng):
                        n = ck * ng + x
                        tt = tg[x]
                        psa, psg = psmm[0], psmm[1]
                        sc.wait_ge(p.sem["g_am"], b["g_am"] + n + 1)
                        if n >= 1:
                            sc.wait_ge(p.sem["g_ff"], b["g_ff"] + n)
                        p.inc(sc.activation(B["f_gb"][:, :], psg[:, 0:512],
                                            AF.Gelu,
                                            bias=B["f_b1c"][:, 2 * tt + 1:2 * tt + 2]),
                              "g_ge")

            @blk.vector
            def _(v):
                for ck in range(4):
                    for x in range(ng):
                        n = ck * ng + x
                        tt = tg[x]
                        psa = psmm[0]
                        v.wait_ge(p.sem["g_ge"], b["g_ge"] + n + 1)
                        if ck >= 1 and x == 0:
                            v.wait_ge(p.sem["g_om"], b["g_om"] + 8 * ck)
                        v.scalar_tensor_tensor(
                            ffb[:, 512 * x:512 * x + 512], psa[:, 0:512],
                            B["f_b1c"][:, 2 * tt:2 * tt + 1], B["f_gb"][:, :],
                            ALU.add, ALU.mult)
                        p.inc(v.drain(), "g_ff")
                    for ts4 in range(4):
                        for dc in range(2):
                            m = ck * 8 + ts4 * 2 + dc
                            ps = psO[m % 2]
                            v.wait_ge(p.sem["g_om"], b["g_om"] + m + 1)
                            aslice = acc[:, D * (4 * ck + ts4) + 512 * dc:
                                         D * (4 * ck + ts4) + 512 * dc + 512]
                            v.tensor_add(aslice, aslice, ps[:, 0:512])
                            p.inc(v.drain(), "g_ac")

    b2s = p.base("o_dx", "o_va", "o_st")
    with nc.Block() as blk:
        @blk.sync
        def _(s):
            for i in range(NT):
                if i >= 2:
                    s.wait_ge(p.sem["o_st"], b2s["o_st"] + 16 * (i - 1))
                p.inc(s.dma_start(out=B["xt2"][:, (i % 2) * D:(i % 2) * D + D],
                                  in_=Xp_mid[1 + 128 * i:129 + 128 * i, :]),
                      "o_dx", 16)
                if i >= 1:
                    s.wait_ge(p.sem["o_va"], b2s["o_va"] + i)
                    p.inc(s.dma_start(
                        out=Xp_out[1 + 128 * (i - 1):129 + 128 * (i - 1), :],
                        in_=B["xt2"][:, ((i - 1) % 2) * D:((i - 1) % 2) * D + D]),
                        "o_st", 16)
            s.wait_ge(p.sem["o_va"], b2s["o_va"] + NT)
            p.inc(s.dma_start(out=Xp_out[1 + 128 * (NT - 1):129 + 128 * (NT - 1), :],
                              in_=B["xt2"][:, ((NT - 1) % 2) * D:((NT - 1) % 2) * D + D]),
                  "o_st", 16)
            s.wait_ge(p.sem["o_st"], b2s["o_st"] + 16 * NT)

        @blk.vector
        def _(v):
            for i in range(NT):
                v.wait_ge(p.sem["o_dx"], b2s["o_dx"] + 16 * (i + 1))
                xo = B["xt2"][:, (i % 2) * D:(i % 2) * D + D]
                v.tensor_add(xo, xo, B["f_b2r"][:, :])
                v.drain()
                v.tensor_add(xo, xo, acc[:, D * i:D * i + D])
                p.inc(v.drain(), "o_va")


def phase_ln_to_hbm(p, X_in, Xp_out, eps=1e-5, extra_zero=()):
    nc = p.nc
    B = p.buf
    xt, junk, cols, tmp = B["xt2"], B["junk"], B["cols"], B["tmp_f32"]
    b = p.base("h_dx", "h_vs", "h_sq", "h_nm", "h_st", "h_z")
    with nc.Block() as blk:
        @blk.vector
        def _(v):
            v.memset(tmp[:, :], 0.0)
            p.inc(v.drain(), "h_z")
            for i in range(NT):
                if i >= 1:
                    v.wait_ge(p.sem["h_nm"], b["h_nm"] + i)
                v.wait_ge(p.sem["h_dx"], b["h_dx"] + 16 * (i + 1))
                xti = xt[:, (i % 2) * D:(i % 2) * D + D]
                v.reduce_sum(cols[:, 0:1], xti, axis=AX.X)
                v.wait_ge(p.sem["h_sq"], b["h_sq"] + i + 1)
                v.drain()
                v.tensor_scalar_mul(cols[:, 2:3], cols[:, 0:1], 1.0 / D)
                v.drain()
                v.tensor_mul(cols[:, 3:4], cols[:, 2:3], cols[:, 2:3])
                v.tensor_scalar(cols[:, 4:5], cols[:, 1:2], 1.0 / D, None, ALU.mult)
                v.drain()
                v.tensor_sub(cols[:, 5:6], cols[:, 4:5], cols[:, 3:4])
                v.drain()
                v.tensor_scalar_add(cols[:, 5:6], cols[:, 5:6], eps)
                v.drain()
                v.reciprocal(cols[:, 6:7], cols[:, 5:6])
                p.inc(v.drain(), "h_vs")

        @blk.scalar
        def _(sc):
            for i in range(NT):
                sc.wait_ge(p.sem["h_dx"], b["h_dx"] + 16 * (i + 1))
                if i >= 1:
                    sc.wait_ge(p.sem["h_vs"], b["h_vs"] + i)
                xti = xt[:, (i % 2) * D:(i % 2) * D + D]
                sc.activation(junk[:, :], xti, AF.Square, accum_out=cols[:, 1:2])
                p.inc(sc.drain(), "h_sq")
                sc.wait_ge(p.sem["h_vs"], b["h_vs"] + i + 1)
                sc.activation(cols[:, 7:8], cols[:, 6:7], AF.Sqrt)
                sc.mul(cols[:, 8:9], cols[:, 2:3], -1.0)
                sc.drain()
                sc.activation(cols[:, 9:10], cols[:, 8:9], AF.Identity,
                              scale=cols[:, 7:8])
                sc.drain()
                if i >= 2:
                    sc.wait_ge(p.sem["h_st"], b["h_st"] + 16 * (i - 1))
                p.inc(sc.activation(tmp[:, :] if False else
                                    B["U16"][:, ((i % 2)) * D:((i % 2)) * D + D],
                                    xti, AF.Identity, scale=cols[:, 7:8],
                                    bias=cols[:, 9:10]), "h_nm")

        @blk.sync
        def _(s):
            s.wait_ge(p.sem["h_z"], b["h_z"] + 1)
            p.inc(s.dma_start(out=Xp_out[0:1, :], in_=tmp[0:1, :]), "h_st", 16)
            nzx = 0
            for za in extra_zero:
                p.inc(s.dma_start(out=za[0:1, :], in_=tmp[0:1, :]), "h_st", 16)
                nzx += 1
            for i in range(NT):
                if i >= 2:
                    s.wait_ge(p.sem["h_nm"], b["h_nm"] + i - 1)
                p.inc(s.dma_start(out=xt[:, (i % 2) * D:(i % 2) * D + D],
                                  in_=X_in[128 * i:128 * i + 128, :]), "h_dx", 16)
                s.wait_ge(p.sem["h_nm"], b["h_nm"] + i + 1)
                p.inc(s.dma_start(out=Xp_out[1 + 128 * i:129 + 128 * i, :],
                                  in_=B["U16"][:, (i % 2) * D:(i % 2) * D + D]),
                      "h_st", 16)
            s.wait_ge(p.sem["h_st"], b["h_st"] + 16 * (NT + 1 + nzx))


def phase_residual_hbm(p, X_old_ap, F_ap, X_new_ap, eps=1e-5):
    nc = p.nc
    B = p.buf
    xo, ft, tmp, junk, cols = B["xt2"], B["U16"], B["tmp_f32"], B["junk"], B["cols"]
    b = p.base("q_dx", "q_df", "q_sq", "q_vs", "q_cb", "q_st", "q_ad")
    with nc.Block() as blk:
        @blk.sync
        def _(s):
            for i in range(NT):
                if i >= 2:
                    s.wait_ge(p.sem["q_st"], b["q_st"] + 16 * (i - 1))
                p.inc(s.dma_start(out=xo[:, (i % 2) * D:(i % 2) * D + D],
                                  in_=X_old_ap[128 * i:128 * i + 128, :]),
                      "q_dx", 16)
                if i >= 2:
                    s.wait_ge(p.sem["q_cb"], b["q_cb"] + i - 1)
                p.inc(s.dma_start(out=ft[:, (i % 2) * D:(i % 2) * D + D],
                                  in_=F_ap[1 + 128 * i:129 + 128 * i, :]),
                      "q_df", 16)
                if i >= 1:
                    s.wait_ge(p.sem["q_ad"], b["q_ad"] + i)
                    p.inc(s.dma_start(
                        out=X_new_ap[128 * (i - 1):128 * i, :],
                        in_=xo[:, ((i - 1) % 2) * D:((i - 1) % 2) * D + D]),
                        "q_st", 16)
            s.wait_ge(p.sem["q_ad"], b["q_ad"] + NT)
            p.inc(s.dma_start(out=X_new_ap[128 * (NT - 1):128 * NT, :],
                              in_=xo[:, ((NT - 1) % 2) * D:((NT - 1) % 2) * D + D]),
                  "q_st", 16)
            s.wait_ge(p.sem["q_st"], b["q_st"] + 16 * NT)

        @blk.scalar
        def _(sc):
            for i in range(NT):
                sc.wait_ge(p.sem["q_df"], b["q_df"] + 16 * (i + 1))
                if i >= 1:
                    sc.wait_ge(p.sem["q_vs"], b["q_vs"] + i)
                fti = ft[:, (i % 2) * D:(i % 2) * D + D]
                sc.activation(junk[:, :], fti, AF.Square, accum_out=cols[:, 1:2])
                p.inc(sc.drain(), "q_sq")
                sc.wait_ge(p.sem["q_vs"], b["q_vs"] + i + 1)
                sc.activation(cols[:, 7:8], cols[:, 6:7], AF.Sqrt)
                sc.mul(cols[:, 8:9], cols[:, 2:3], -1.0)
                sc.drain()
                sc.activation(cols[:, 9:10], cols[:, 8:9], AF.Identity,
                              scale=cols[:, 7:8])
                sc.drain()
                if i >= 1:
                    sc.wait_ge(p.sem["q_ad"], b["q_ad"] + i)
                p.inc(sc.activation(tmp[:, :], fti, AF.Identity,
                                    scale=cols[:, 7:8], bias=cols[:, 9:10]),
                      "q_cb")

        @blk.vector
        def _(v):
            for i in range(NT):
                v.wait_ge(p.sem["q_df"], b["q_df"] + 16 * (i + 1))
                if i >= 1:
                    v.wait_ge(p.sem["q_cb"], b["q_cb"] + i)
                fti = ft[:, (i % 2) * D:(i % 2) * D + D]
                v.reduce_sum(cols[:, 0:1], fti, axis=AX.X)
                v.wait_ge(p.sem["q_sq"], b["q_sq"] + i + 1)
                v.drain()
                v.tensor_scalar_mul(cols[:, 2:3], cols[:, 0:1], 1.0 / D)
                v.drain()
                v.tensor_mul(cols[:, 3:4], cols[:, 2:3], cols[:, 2:3])
                v.tensor_scalar(cols[:, 4:5], cols[:, 1:2], 1.0 / D, None, ALU.mult)
                v.drain()
                v.tensor_sub(cols[:, 5:6], cols[:, 4:5], cols[:, 3:4])
                v.drain()
                v.tensor_scalar_add(cols[:, 5:6], cols[:, 5:6], eps)
                v.drain()
                v.reciprocal(cols[:, 6:7], cols[:, 5:6])
                p.inc(v.drain(), "q_vs")
                v.wait_ge(p.sem["q_cb"], b["q_cb"] + i + 1)
                v.wait_ge(p.sem["q_dx"], b["q_dx"] + 16 * (i + 1))
                xoi = xo[:, (i % 2) * D:(i % 2) * D + D]
                v.tensor_add(xoi, xoi, tmp[:, :])
                p.inc(v.drain(), "q_ad")


def emit_sb3(p, X_in, X_out, aps, scr):
    p.mark("3c")
    phase_sb3_consts(p, aps)
    p.mark("3lnh")
    phase_ln_to_hbm(p, X_in, scr["XA"], extra_zero=(scr["XB"], scr["XC"]))
    cur = "XA"
    for l in range(2):
        mid = "XB" if cur != "XB" else "XC"
        out = [k for k in ("XA", "XB", "XC") if k not in (cur, mid)][0]
        with p.scope():
            alloc_sb3_attn(p)
            p.mark(f"3ac{l}")
            phase_sb3_attn_consts(p, aps, l)
            p.mark(f"3sh{l}")
            phase_sb3_shiftT(p, scr[cur])
            p.mark(f"3qkv{l}")
            phase_sb3_qkv(p, aps, l)
            p.mark(f"3at{l}")
            phase_sb3_attn(p, scr[cur], scr[mid])
        with p.scope():
            alloc_sb3_ffnA(p)
            p.mark(f"3fc{l}")
            phase_sb3_ffn_consts(p, aps, l)
            with p.scope():
                alloc_sb3_hnT(p)
                phase_sb3_hnT_consts(p, aps, l)
                p.mark(f"3hnT{l}")
                phase_sb3_hnT(p, scr[mid])
            with p.scope():
                alloc_sb3_w(p)
                p.mark(f"3ffn{l}")
                phase_sb3_ffn(p, aps, l, scr[mid], scr[out])
        cur = out
    p.mark("3res")
    phase_residual_hbm(p, X_in, scr[cur], X_out)


def sb3_consts():
    DIM_HEAD, WINDOW = 64, 512
    n = SEQ
    half_idx = np.arange(0, DIM_HEAD, 2, dtype=np.float64)
    inv_freq = 1.0 / 10000.0 ** (half_idx / DIM_HEAD)
    t = np.arange(n, dtype=np.float64)
    freqs = t[:, None] * inv_freq[None, :]
    freqs = np.concatenate([freqs, freqs], axis=-1)
    base = (half_idx + 0.4 * DIM_HEAD) / (1.4 * DIM_HEAD)
    power = (t - n // 2) / float(WINDOW // 2)
    sc = base[None, :] ** power[:, None]
    sc = np.concatenate([sc, sc], axis=-1)
    cos, sin = np.cos(freqs), np.sin(freqs)
    qq = np.arange(128)
    return {
        "C_CSQ": (cos * sc).astype(np.float32),
        "C_SNQ": (sin * sc).astype(np.float32),
        "C_CSK": (cos / sc).astype(np.float32),
        "C_SNK": (sin / sc).astype(np.float32),
        "C_MASKD": np.where(qq[None, :] > qq[:, None], np.float32(-1e30),
                            np.float32(0.0)),
        "C_MASKL": np.where(qq[None, :] < qq[:, None], np.float32(-1e30),
                            np.float32(0.0)),
    }


def sb3_pack_ffn(w1, b1, w2):
    HID = 2730
    w1p = np.zeros((2, 1024, 22 * 256), dtype=np.float32)
    b1p = np.zeros((2, 22 * 256), dtype=np.float32)
    w2p = np.zeros((2, 22 * 128, 1024), dtype=np.float32)
    for l in range(2):
        for t in range(22):
            r0, r1 = 128 * t, min(128 * t + 128, HID)
            w = r1 - r0
            w1p[l][:, 256 * t:256 * t + w] = w1[l][:, r0:r1]
            w1p[l][:, 256 * t + 128:256 * t + 128 + w] = w1[l][:, HID + r0:HID + r1]
            b1p[l][256 * t:256 * t + w] = b1[l][r0:r1]
            b1p[l][256 * t + 128:256 * t + 128 + w] = b1[l][HID + r0:HID + r1]
            w2p[l][128 * t:128 * t + w] = w2[l][r0:r1]
    b1c = np.ascontiguousarray(b1p.reshape(2, 44, 128).transpose(0, 2, 1))
    return w1p, b1c, w2p


def build_sb3_program():
    nc = bass.Bass("TRN2", target_bir_lowering=False, debug=False)
    X = nc.dram_tensor("X", [SEQ, D], F32, kind="ExternalInput").ap()
    aps = {}
    for nm, shp in (("wqkv", [2, D, 1536]), ("q_scale", [2, 64]),
                    ("k_scale", [2, 64]), ("wo", [2, 512, D]),
                    ("ln_g", [2, D]), ("ln_b", [2, D]),
                    ("w1p", [2, D, 5632]), ("b1p", [2, 128, 44]),
                    ("w2p", [2, 2816, D]), ("b2", [2, D])):
        aps[nm] = nc.dram_tensor(nm, shp, F32, kind="ExternalInput").ap()
    for nm, shp in (("C_EYE", [128, 128]), ("C_CSQ", [SEQ, 64]),
                    ("C_SNQ", [SEQ, 64]), ("C_CSK", [SEQ, 64]),
                    ("C_SNK", [SEQ, 64]), ("C_MASKD", [128, 128]),
                    ("C_MASKL", [128, 128])):
        aps[nm] = nc.dram_tensor(nm, shp, F32, kind="ExternalInput").ap()
    Y = nc.dram_tensor("Y", [SEQ, D], F32, kind="ExternalOutput").ap()
    scr = {k: nc.dram_tensor("SCR" + k, [SEQ + 1, D], F32, kind="Internal").ap()
           for k in ("XA", "XB", "XC")}
    with ExitStack() as ctx:
        p = P(nc, ctx)
        alloc_common(p)
        alloc_sb3_outer(p)
        emit_sb3(p, X, Y, aps, scr)
    return nc


def run_sb3_dev(x_seqs, W):
    n = x_seqs.shape[0]
    nc = build_sb3_program()
    w1p, b1p, w2p = sb3_pack_ffn(W["L1_ffn_w1"], W["L1_ffn_b1"], W["L1_ffn_w2"])
    consts = sb3_consts()
    consts["C_EYE"] = np.eye(128, dtype=np.float32)
    in_maps = []
    for i in range(n):
        im = {"X": np.ascontiguousarray(x_seqs[i]),
              "wqkv": W["L1_ffn_wqkv"], "q_scale": W["L1_ffn_q_scale"],
              "k_scale": W["L1_ffn_k_scale"], "wo": W["L1_ffn_wo"],
              "ln_g": W["L1_ffn_ln_g"], "ln_b": W["L1_ffn_ln_b"],
              "w1p": w1p, "b1p": b1p, "w2p": w2p, "b2": W["L1_ffn_b2"]}
        im.update(consts)
        in_maps.append(im)
    res = run_bass_kernel_spmd(nc, in_maps, core_ids=list(range(n)))
    return np.stack([r["Y"] for r in res.results])


def alloc_sb1_outer(p):
    p.sb("m_kw", [128, 1024], BF16)
    p.sb("m_vw", [128, 1024], BF16)
    p.sb("m_rtT", [128, 132], BF16)
    p.sb("m_rtc", [128, 2], F32)
    p.sb("m_kb", [128, 1], F32)
    p.sb("m_vb", [1, 128], BF16)
    p.sb("m_qbT", [128, 8], F32)
    p.sb("m_sqb", [128, 8], F32)
    p.sb("m_ob", [8, 1024], BF16)
    p.sb("m_l1g", [128, D], F32)
    p.sb("m_l1b", [128, D], F32)
    p.sb("m_glog", [128, NT * 16], F32)
    p.sb("m_g2", [128, NT * 16], F32)
    p.sb("m_gT", [8, SEQ], BF16)
    p.sb("m_g2T", [16, SEQ], BF16)
    p.sb("m_b2bf", [16, 1024], BF16)
    p.sb("m_fb1", [128, 64], F32)
    p.sb("ones1", [1, 128], BF16)
    p.sb("idf32b", [128, 128], F32)
    p.sb("m_zc", [128, 1], F32)


def alloc_sb1_lnT(p):
    p.sb("m_xaf", [128, D], F32)
    p.sb("m_xf", [128, D], F32)
    p.sb("m_gwf", [128, 128], F32)


def alloc_sb1_moa(p):
    p.sb("m_kT", [128, SEQ], BF16)
    p.sb("m_kTlo", [128, SEQ], BF16)
    p.sb("m_kThi", [128, SEQ], BF16)
    p.sb("m_vx", [128, NT * 130], BF16)
    p.sb("m_qT", [128, 1024], BF16)
    p.sb("m_acc", [128, 4 * D], F32)
    p.sb("m_clo", [64, 512], BF16)
    p.sb("m_chi", [64, 512], BF16)
    p.sb("m_z", [1, 512], F32)
    p.sb("m_rz", [128, 4], F32)
    p.sb("m_se", [128, 4], F32)
    p.sb("m_pad", [128, 2560], F32)
    p.sb("m_band", [128, 2 * 512], BF16)


def alloc_sb1_moe(p):
    p.sb("m_l2g", [128, D], F32)
    p.sb("m_l2b", [128, D], F32)
    p.sb("e_acc", [128, NT * D], F32)
    p.sb("e_w1", [128, 8 * 512], BF16)
    p.sb("e_w2", [128, 4 * 1024], BF16)
    p.sb("e_hT", [128, 4 * 1024], BF16)


def phase_sb1_consts(p, aps):
    nc = p.nc
    B = p.buf
    b = p.base("c_dc")
    with nc.Block() as blk:
        @blk.gpsimd
        def _(g):
            p.inc(g.dma_start(out=B["idbf"][:, :], in_=aps["C_EYE"]), "c_dc", 16)
            p.inc(g.dma_start(out=B["m_kw"][:, :].rearrange("p (a n) -> p a n", a=8),
                              in_=aps["k_w"].rearrange("(a p) n -> p a n", p=128)),
                  "c_dc", 16)
            p.inc(g.dma_start(out=B["m_vw"][:, :].rearrange("p (a n) -> p a n", a=8),
                              in_=aps["v_w"].rearrange("(a p) n -> p a n", p=128)),
                  "c_dc", 16)
            p.inc(g.dma_start(out=B["m_rtT"][:, 0:129], in_=aps["rtT"]), "c_dc", 16)
            p.inc(g.dma_start(out=B["m_vb"][:, :],
                              in_=aps["v_b"].rearrange("(a d) -> a d", a=1)), "c_dc", 16)
            p.inc(g.dma_start(out=B["m_ob"][:, :], in_=aps["o_b"]), "c_dc", 16)
            p.inc(g.dma_start(out=B["m_b2bf"][:, :], in_=aps["ff_b2"]), "c_dc", 16)

        @blk.vector
        def _(v):
            v.memset(B["idf32"][:, :], 1.0)
            v.memset(B["cst"][:, 1:2], -30.0)
            v.memset(B["ones1"][:, :], 1.0)
            v.memset(B["m_zc"][:, :], 0.0)

        @blk.sync
        def _(s):
            p.inc(s.dma_start(out=B["m_rtc"][:, 0:1], in_=aps["rtT"][:, 0:1], allow_slow_non_contiguous=True), "c_dc", 16)
            p.inc(s.dma_start(out=B["m_rtc"][:, 1:2], in_=aps["rtT"][:, 128:129], allow_slow_non_contiguous=True), "c_dc", 16)
            p.inc(s.dma_start(out=B["m_kb"][:, :],
                              in_=aps["k_b"].rearrange("(p a) -> p a", a=1)), "c_dc", 16)
            p.inc(s.dma_start(out=B["m_qbT"][:, :], in_=aps["qbT"]), "c_dc", 16)
            p.inc(s.dma_start(out=B["m_l1g"][:, :],
                              in_=aps["ln1_w"].rearrange("(a d) -> a d", a=1).to_broadcast([128, D])),
                  "c_dc", 16)
            p.inc(s.dma_start(out=B["m_l1b"][:, :],
                              in_=aps["ln1_b"].rearrange("(a d) -> a d", a=1).to_broadcast([128, D])),
                  "c_dc", 16)
            p.inc(s.dma_start(out=B["m_fb1"][:, :], in_=aps["ffb1c"]), "c_dc", 16)
            p.inc(s.dma_start(out=B["idf32b"][:, :], in_=aps["C_EYE"]), "c_dc", 16)
            s.wait_ge(p.sem["c_dc"], b["c_dc"] + 15 * 16)

    b2 = p.base("c_v2")
    with nc.Block() as blk:
        @blk.vector
        def _(v):
            v.tensor_scalar_mul(B["m_sqb"][:, :], B["m_qbT"][:, :], 0.088388348)
            p.inc(v.drain(), "c_v2")


def phase_sb1_kv(p):
    nc = p.nc
    B = p.buf
    xnT = B["xnT"]
    psmm = [B["psA"], B["psB"]]
    vx_v = B["m_vx"][:, :].rearrange("p (t c) -> p t c", t=NT)
    b = p.base("k_mm", "k_ev")
    with nc.Block() as blk:
        @blk.tensor
        def _(t):
            for c in range(4):
                ps = psmm[c % 2]
                if c >= 2:
                    t.wait_ge(p.sem["k_ev"], b["k_ev"] + c - 1)
                for kd in range(ND):
                    ins = t.matmul(ps[:, 0:512],
                                   B["m_kw"][:, 128 * kd:128 * kd + 128],
                                   xnT[:, kd * SEQ + 512 * c:kd * SEQ + 512 * c + 512],
                                   start=(kd == 0), stop=(kd == ND - 1))
                p.inc(ins, "k_mm")
            for tt in range(NT):
                n = 4 + tt
                ps = psmm[n % 2]
                t.wait_ge(p.sem["k_ev"], b["k_ev"] + n - 1)
                for kd in range(ND):
                    t.matmul(ps[:, 0:128],
                             xnT[:, kd * SEQ + 128 * tt:kd * SEQ + 128 * tt + 128],
                             B["m_vw"][:, 128 * kd:128 * kd + 128],
                             start=(kd == 0), stop=False)
                ins = t.matmul(ps[:, 0:128], B["ones1"][0:1, :], B["m_vb"][0:1, :],
                               start=False, stop=True)
                p.inc(ins, "k_mm")

        @blk.scalar
        def _(sc):
            for c in range(4):
                ps = psmm[c % 2]
                sc.wait_ge(p.sem["k_mm"], b["k_mm"] + c + 1)
                p.inc(sc.activation(B["m_kT"][:, 512 * c:512 * c + 512],
                                    ps[:, 0:512], AF.Identity,
                                    bias=B["m_kb"][:, 0:1]), "k_ev")
            for tt in range(NT):
                n = 4 + tt
                ps = psmm[n % 2]
                sc.wait_ge(p.sem["k_mm"], b["k_mm"] + n + 1)
                srcap = bass.AP(ps[:, :].tensor, ps[:, :].offset,
                                [ps[:, :].ap[0], [64, 2], [1, 64]])
                d0 = vx_v[:, tt, :]
                dstap = bass.AP(d0.tensor, d0.offset,
                                [d0.ap[0], [65, 2], [1, 64]])
                p.inc(sc.activation(dstap, srcap, AF.Copy), "k_ev")

    b2 = p.base("k_v2")
    with nc.Block() as blk:
        @blk.vector
        def _(v):
            v.tensor_scalar(B["m_kTlo"][:, :], B["m_kT"][:, :],
                            B["m_rtc"][:, 0:1], None, ALU.add)
            v.tensor_scalar(B["m_kThi"][:, :], B["m_kT"][:, :],
                            B["m_rtc"][:, 1:2], None, ALU.add)
            v.memset(vx_v[:, :, 64:65], 1.0)
            v.memset(vx_v[:, :, 129:130], 1.0)
            p.inc(v.drain(), "k_v2")


def _top2(p, v, sc_todo, glog, gout, ne, tt, b, semname):
    pass


def phase_sb1_gates(p, which):
    nc = p.nc
    B = p.buf
    ne = 8 if which == "moa" else 16
    glog = B["m_glog"] if which == "moa" else B["m_g2"]
    gT = B["m_gT"] if which == "moa" else B["m_g2T"]
    cols = B["cols"]
    b = p.base("t_v1", "t_s1", "t_v2", "t_tt", "t_ev")
    psmm = B["psC"]
    with nc.Block() as blk:
        @blk.vector
        def _(v):
            for tt in range(NT):
                gl = glog[:, 16 * tt:16 * tt + ne]
                if tt >= 1:
                    v.wait_ge(p.sem["t_s1"], b["t_s1"] + tt)
                v.reduce_max(cols[:, 0:1], gl, axis=AX.X, negate=True)
                v.drain()
                v.tensor_scalar_mul(cols[:, 4:5], cols[:, 0:1], -1.0)
                v.drain()
                v.tensor_scalar(cols[:, 5:6] if False else B["tmp_f32"][:, 0:ne],
                                gl, cols[:, 4:5], None, ALU.is_ge)
                v.drain()
                v.scalar_tensor_tensor(B["tmp_f32"][:, 16:16 + ne],
                                       B["tmp_f32"][:, 0:ne], -1e30, gl,
                                       ALU.mult, ALU.add)
                v.drain()
                v.reduce_max(cols[:, 1:2], B["tmp_f32"][:, 16:16 + ne],
                             axis=AX.X, negate=True)
                v.drain()
                v.tensor_sub(cols[:, 2:3], cols[:, 1:2], cols[:, 0:1])
                v.drain()
                v.tensor_sub(cols[:, 2:3], cols[:, 0:1], cols[:, 1:2])
                p.inc(v.drain(), "t_v1")
                v.wait_ge(p.sem["t_s1"], b["t_s1"] + tt + 1)
                v.tensor_scalar_add(cols[:, 6:7], cols[:, 5:6], 1.0)
                v.drain()
                v.reciprocal(cols[:, 6:7], cols[:, 6:7])
                v.drain()
                v.tensor_scalar_mul(cols[:, 7:8], cols[:, 1:2], -1.0)
                v.drain()
                v.tensor_scalar(B["tmp_f32"][:, 0:ne], gl, cols[:, 7:8],
                                None, ALU.is_ge)
                v.drain()
                v.tensor_mul(gl, B["tmp_f32"][:, 0:ne], B["tmp_f32"][:, 32:32 + ne])
                v.drain()
                p.inc(v.tensor_scalar(gl, gl, cols[:, 6:7], None, ALU.mult),
                      "t_v2")

        @blk.scalar
        def _(sc):
            for tt in range(NT):
                gl = glog[:, 16 * tt:16 * tt + ne]
                sc.wait_ge(p.sem["t_v1"], b["t_v1"] + tt + 1)
                sc.activation(B["tmp_f32"][:, 32:32 + ne], gl, AF.Exp,
                              bias=cols[:, 0:1])
                sc.activation(cols[:, 5:6], cols[:, 2:3], AF.Exp)
                p.inc(sc.drain(), "t_s1")
                if tt >= 1:
                    sc.wait_ge(p.sem["t_tt"], b["t_tt"] + tt)
                    p.inc(sc.activation(
                        gT[0:ne, 128 * (tt - 1):128 * tt],
                        psmm[0:ne, 128 * ((tt - 1) % 4):128 * ((tt - 1) % 4) + 128],
                        AF.Copy), "t_ev")
            sc.wait_ge(p.sem["t_tt"], b["t_tt"] + NT)
            p.inc(sc.activation(
                gT[0:ne, 128 * (NT - 1):128 * NT],
                psmm[0:ne, 128 * ((NT - 1) % 4):128 * ((NT - 1) % 4) + 128],
                AF.Copy), "t_ev")

        @blk.tensor
        def _(t):
            for tt in range(NT):
                t.wait_ge(p.sem["t_v2"], b["t_v2"] + tt + 1)
                if tt >= 3:
                    t.wait_ge(p.sem["t_ev"], b["t_ev"] + tt - 2)
                ins = t.matmul(psmm[0:ne, 128 * (tt % 4):128 * (tt % 4) + 128],
                               glog[:, 16 * tt:16 * tt + ne],
                               B["idf32b"][:, :], start=True, stop=True)
                p.inc(ins, "t_tt")


def phase_sb1_moa_qc(p, aps, qc, XS0, XA, RELP):
    nc = p.nc
    B = p.buf
    xnT, kT, kTlo, kThi = B["xnT"], B["m_kT"], B["m_kTlo"], B["m_kThi"]
    vx_v = B["m_vx"][:, :].rearrange("p (t c) -> p t c", t=NT)
    acc = B["m_acc"]
    RELPs = RELP if isinstance(RELP, (tuple, list)) else (RELP, RELP)
    dbl = RELPs[0] is not RELPs[1]
    psS = [B["psA"], B["psB"]]
    psCL = B["psC"]
    psCH = B["psTA"][:, 0:512]
    psQ = B["psTA"][:, 512:1024]
    psW = B["psTB"][:, 0:512]
    psZ = B["psZ"]
    pts = [B["pt0"], B["pt1"]]
    EA = 8
    bandkts = [kt for kt in range(NT) if -1 <= kt - 4 * qc <= 4]
    satkts = [kt for kt in range(NT) if kt not in bandkts]
    tiles = satkts + bandkts
    nsat, nb = len(satkts), len(bandkts)
    uqw = B["wg"]
    qw = [uqw[:, 0:1024], uqw[:, 1024:2048]]
    uow = B["U16"][:, :].bitcast(BF16)[0:64, :]
    ow = [uow[:, 4096:6144], uow[:, 6144:8192]]
    qT = [B["m_qT"][:, 0:512], B["m_qT"][:, 512:1024]]
    pad = [B["m_pad"][:, 0:1280], B["m_pad"][:, 1280:2560]]
    band = [B["m_band"][:, 0:512], B["m_band"][:, 512:1024]]

    b = p.base("e_qw", "e_ow", "e_qm", "e_qe", "e_rm", "e_rc", "e_rp",
               "e_rd", "e_bd", "e_ts", "e_se", "e_tc", "e_zr", "e_ce",
               "e_tz", "e_vz", "e_wm", "e_va")

    def W(eng, nm, v):
        if v > 0:
            eng.wait_ge(p.sem[nm], b[nm] + v)

    with nc.Block() as blk:
        @blk.gpsimd
        def _(g):
            def ldw(e):
                W(g, "e_qm", e - 1)
                p.inc(g.dma_start(
                    out=qw[e % 2].rearrange("p (a n) -> p a n", a=8),
                    in_=aps["q_w"][e].rearrange("(a p) n -> p a n", p=128)),
                    "e_qw", 16)
                W(g, "e_wm", 8 * (e - 1))
                p.inc(g.dma_start(
                    out=ow[e % 2].rearrange("p (a n) -> p a n", a=2),
                    in_=aps["o_w"][e].rearrange("(a p) n -> p a n", p=64)),
                    "e_ow", 16)
            ldw(0)
            ldw(1)
            for e in range(EA):
                RP = RELPs[e % 2]
                for x, kt in enumerate(bandkts):
                    W(g, "e_rd", 64 * (e + 1))
                    bg = nb * e + x
                    if bg >= 2:
                        eb, xb = (bg - 2) // nb, (bg - 2) % nb
                        W(g, "e_ts", 16 * eb + nsat + xb + 1)
                    off = (512 * qc) * 1279 + 128 * kt + 639
                    src = bass.AP(RP.tensor, RP.offset + off,
                                  [[1279, 128], [128 * 1279, 4], [1, 128]])
                    p.inc(g.dma_start(
                        out=band[bg % 2].rearrange("p (a n) -> p a n", a=4),
                        in_=src), "e_bd", 16)
                if e + 2 < EA:
                    ldw(e + 2)

        @blk.tensor
        def _(t):
            def qproj(e):
                W(t, "e_qw", 16 * (e + 1))
                W(t, "e_qe", e)
                for kd in range(ND):
                    ins = t.matmul(psQ, qw[e % 2][:, 128 * kd:128 * kd + 128],
                                   xnT[:, kd * SEQ + 512 * qc:
                                       kd * SEQ + 512 * qc + 512],
                                   start=(kd == 0), stop=(kd == ND - 1))
                p.inc(ins, "e_qm")

            qproj(0)
            for e in range(EA):
                for j in range(4):
                    W(t, "e_qe", e + 1)
                    if j == 0:
                        W(t, "e_ce", 2 * e)
                    else:
                        W(t, "e_rc", 4 * e + j)
                    p.inc(t.matmul(psCL[:, 0:129],
                                   qT[e % 2][:, 128 * j:128 * j + 128],
                                   B["m_rtT"][:, 0:129], start=True,
                                   stop=True), "e_rm")

                def mm_s(pos, kt):
                    d = kt - 4 * qc
                    ps = psS[pos % 2]
                    W(t, "e_se", 16 * e + pos - 1)
                    kv = kTlo if d <= -2 else (kThi if d >= 5 else kT)
                    isband = kv is kT
                    ins = t.matmul(ps[:, 0:512], kv[:, 128 * kt:128 * kt + 128],
                                   qT[e % 2][:, :], start=True, stop=not isband)
                    if isband:
                        bg = nb * e + bandkts.index(kt)
                        W(t, "e_bd", 16 * (bg + 1))
                        for jq in range(4):
                            ins = t.matmul(
                                ps[:, 128 * jq:128 * jq + 128],
                                band[bg % 2][:, 128 * jq:128 * jq + 128],
                                B["idbf"][:, :], start=False, stop=(jq == 3))
                    p.inc(ins, "e_ts")

                def ctx(pos, kt):
                    W(t, "e_se", 16 * e + pos + 1)
                    p.inc(t.matmul(psCL[0:65, 0:512], vx_v[:, kt, 0:65],
                                   pts[pos % 2][:, :], start=(pos == 0),
                                   stop=(pos == NT - 1)), "e_tc")
                    p.inc(t.matmul(psCH[0:65, 0:512], vx_v[:, kt, 65:130],
                                   pts[pos % 2][:, :], start=(pos == 0),
                                   stop=(pos == NT - 1)), "e_tc")

                for pos, kt in enumerate(tiles):
                    mm_s(pos, kt)
                    if pos >= 1:
                        ctx(pos - 1, tiles[pos - 1])
                ctx(NT - 1, tiles[NT - 1])
                W(t, "e_zr", e + 1)
                W(t, "e_vz", e)
                for j in range(4):
                    ins = t.matmul(psZ[:, j:j + 1],
                                   B["m_z"][0:1, 128 * j:128 * j + 128],
                                   B["idf32"][0:1, 0:1], start=True, stop=True)
                p.inc(ins, "e_tz")
                if e + 1 < EA:
                    qproj(e + 1)
                W(t, "e_ce", 2 * (e + 1))
                W(t, "e_ow", 16 * (e + 1))
                n = 0
                for j in range(4):
                    for dc in range(2):
                        W(t, "e_va", 8 * e + n)
                        ins = t.matmul(psW,
                                       B["m_clo"][0:64, 128 * j:128 * j + 128],
                                       ow[e % 2][:, 512 * dc:512 * dc + 512],
                                       start=True, stop=False)
                        ins = t.matmul(psW,
                                       B["m_chi"][0:64, 128 * j:128 * j + 128],
                                       ow[e % 2][:, 1024 + 512 * dc:
                                                 1024 + 512 * dc + 512],
                                       start=False, stop=True)
                        p.inc(ins, "e_wm")
                        n += 1

        @blk.scalar
        def _(sc):
            for e in range(EA):
                W(sc, "e_qm", e + 1)
                W(sc, "e_ts", 16 * (e - 1))
                p.inc(sc.activation(qT[e % 2][:, :], psQ, AF.Identity,
                                    scale=0.088388348,
                                    bias=B["m_sqb"][:, e:e + 1]), "e_qe")
                for j in range(4):
                    W(sc, "e_rm", 4 * e + j + 1)
                    if j == 0:
                        W(sc, "e_rd", 64 * (e - 1))
                    else:
                        W(sc, "e_rd", 64 * (e - 1) + 16 * j)
                    p.inc(sc.activation(pad[e % 2][:, 575:704], psCL[:, 0:129],
                                        AF.Copy), "e_rc")
                    sc.drain()
                    sc.activation(pad[e % 2][:, 0:575], B["m_l1g"][:, 0:575],
                                  AF.Identity, scale=B["m_zc"][:, 0:1],
                                  bias=pad[e % 2][:, 575:576])
                    p.inc(sc.activation(pad[e % 2][:, 704:1279],
                                        B["m_l1g"][:, 0:575], AF.Identity,
                                        scale=B["m_zc"][:, 0:1],
                                        bias=pad[e % 2][:, 703:704]), "e_rp")
                for pos in range(NT):
                    W(sc, "e_ts", 16 * e + pos + 1)
                    W(sc, "e_tc", 2 * (16 * e + pos - 1))
                    p.inc(sc.activation(pts[pos % 2][:, :],
                                        psS[pos % 2][:, 0:512], AF.Exp,
                                        bias=B["cst"][:, 1:2]), "e_se")


        @blk.vector
        def _(v):
            v.memset(acc[:, :], 0.0)
            v.drain()
            for e in range(EA):
                W(v, "e_tc", 32 * (e + 1))
                v.tensor_copy(B["m_z"][0:1, :], psCL[64:65, 0:512])
                p.inc(v.drain(), "e_zr")
                W(v, "e_wm", 8 * e)
                v.tensor_copy(B["m_clo"][0:64, :], psCL[0:64, 0:512])
                v.tensor_copy(B["m_chi"][0:64, :], psCH[0:64, 0:512])
                p.inc(v.drain(), "e_ce", 2)
                W(v, "e_tz", e + 1)
                v.reciprocal(B["m_rz"][:, :], psZ[:, 0:4])
                v.drain()
                gv = B["m_glog"][:, :].rearrange("p (t c) -> p t c", t=NT)
                gsl = bass.AP(gv.tensor, gv.offset + 16 * 4 * qc + e,
                              [gv.ap[0], [16, 4], [1, 1]])
                v.tensor_tensor(B["m_se"][:, :].rearrange("p (a c) -> p a c", c=1),
                                gsl, B["m_rz"][:, :].rearrange("p (a c) -> p a c", c=1),
                                ALU.mult)
                p.inc(v.drain(), "e_vz")
                for n in range(8):
                    j, dc = n // 2, n % 2
                    W(v, "e_wm", 8 * e + n + 1)
                    asl = acc[:, D * j + 512 * dc:D * j + 512 * dc + 512]
                    v.scalar_tensor_tensor(asl, psW, B["m_se"][:, j:j + 1],
                                           asl, ALU.mult, ALU.add)
                    p.inc(v.drain(), "e_va")

        @blk.sync
        def _(s):
            for e in range(EA):
                RP = RELPs[e % 2]
                for j in range(4):
                    W(s, "e_rp", 4 * e + j + 1)
                    if j == 0:
                        W(s, "e_bd", 16 * nb * (e - 1 if dbl else e))
                    r0 = 512 * qc + 128 * j
                    p.inc(s.dma_start(out=RP[r0:r0 + 128, :],
                                      in_=pad[e % 2][:, 0:1280]), "e_rd", 16)

    b = p.base("a_om", "a_dx", "a_vv", "a_st")
    with nc.Block() as blk:
        @blk.tensor
        def _(t):
            for j in range(4):
                for dc in range(2):
                    n = 2 * j + dc
                    ps = [B["psA"], B["psB"]][n % 2]
                    if n >= 2:
                        jp, dp = (n - 2) // 2, (n - 2) % 2
                        t.wait_ge(p.sem["a_vv"], b["a_vv"] + 3 * jp + dp + 1)
                    tt = 4 * qc + j
                    ins = t.matmul(ps[:, 0:512],
                                   B["m_gT"][0:8, 128 * tt:128 * tt + 128],
                                   B["m_ob"][0:8, 512 * dc:512 * dc + 512],
                                   start=True, stop=True)
                    p.inc(ins, "a_om")

        @blk.sync
        def _(s):
            for j in range(4):
                tt = 4 * qc + j
                if j >= 2:
                    s.wait_ge(p.sem["a_st"], b["a_st"] + 16 * (j - 1))
                p.inc(s.dma_start(out=B["xt2"][:, (j % 2) * D:(j % 2) * D + D],
                                  in_=XS0[1 + 128 * tt:129 + 128 * tt, :]),
                      "a_dx", 16)
                s.wait_ge(p.sem["a_vv"], b["a_vv"] + 3 * (j + 1))
                p.inc(s.dma_start(out=XA[1 + 128 * tt:129 + 128 * tt, :],
                                  in_=B["xt2"][:, (j % 2) * D:(j % 2) * D + D]),
                      "a_st", 16)
            s.wait_ge(p.sem["a_st"], b["a_st"] + 64)

        @blk.vector
        def _(v):
            for j in range(4):
                for dc in range(2):
                    n = 2 * j + dc
                    ps = [B["psA"], B["psB"]][n % 2]
                    v.wait_ge(p.sem["a_om"], b["a_om"] + n + 1)
                    asl = acc[:, D * j + 512 * dc:D * j + 512 * dc + 512]
                    v.tensor_add(asl, asl, ps[:, 0:512])
                    p.inc(v.drain(), "a_vv")
                v.wait_ge(p.sem["a_dx"], b["a_dx"] + 16 * (j + 1))
                xo = B["xt2"][:, (j % 2) * D:(j % 2) * D + D]
                v.tensor_add(xo, xo, acc[:, D * j:D * j + D])
                p.inc(v.drain(), "a_vv")


def _ow_rows(ow, r0, c0):
    return ow[r0:r0 + 64, c0:c0 + 512]


def phase_sb1_moe(p, aps, XA):
    nc = p.nc
    B = p.buf
    xnT, acc = B["xnT"], B["e_acc"]
    psmm = [B["psA"], B["psB"]]
    psO = [B["psTA"], B["psTB"]]
    EF = 16
    bz = p.base("f_z")
    with nc.Block() as blk:
        @blk.vector
        def _(v):
            v.memset(acc[:, :], 0.0)
            p.inc(v.drain(), "f_z")

    for e in range(EF):
        b = p.base("x_w1", "x_w2", "x_hm", "x_he", "x_om", "x_oa")
        with nc.Block() as blk:
            @blk.gpsimd
            def _(g):
                p.inc(g.dma_start(
                    out=B["e_w1"][:, :].rearrange("p (a n) -> p a n", a=8),
                    in_=aps["ff_w1"][e].rearrange("(a p) n -> p a n", p=128)),
                    "x_w1", 16)
                p.inc(g.dma_start(
                    out=B["e_w2"][:, :].rearrange("p (a n) -> p a n", a=4),
                    in_=aps["ff_w2"][e].rearrange("(a p) n -> p a n", p=128)),
                    "x_w2", 16)

            @blk.tensor
            def _(t):
                t.wait_ge(p.sem["x_w1"], b["x_w1"] + 16)
                for half in range(2):
                    for m in range(4):
                        for c in range(2):
                            n = 8 * half + 2 * m + c
                            ps = psmm[n % 2]
                            if n >= 2:
                                t.wait_ge(p.sem["x_he"], b["x_he"] + n - 1)
                            ck = 2 * half + c
                            for kd in range(ND):
                                ins = t.matmul(
                                    ps[:, 0:512],
                                    B["e_w1"][:, 512 * kd + 128 * m:512 * kd + 128 * m + 128],
                                    xnT[:, kd * SEQ + 512 * ck:kd * SEQ + 512 * ck + 512],
                                    start=(kd == 0), stop=(kd == ND - 1))
                            p.inc(ins, "x_hm")
                    t.wait_ge(p.sem["x_w2"], b["x_w2"] + 16)
                    for ts4 in range(8):
                        for dc in range(2):
                            n2 = 16 * half + 2 * ts4 + dc
                            ps = psO[n2 % 2]
                            if n2 >= 2:
                                t.wait_ge(p.sem["x_oa"], b["x_oa"] + n2 - 1)
                            t.wait_ge(p.sem["x_he"], b["x_he"] + 8 * half + 8)
                            for m in range(4):
                                ins = t.matmul(
                                    ps[:, 0:512],
                                    B["e_hT"][:, 1024 * m + 128 * ts4:1024 * m + 128 * ts4 + 128],
                                    B["e_w2"][:, 1024 * m + 512 * dc:1024 * m + 512 * dc + 512],
                                    start=(m == 0), stop=(m == 3))
                            p.inc(ins, "x_om")

            @blk.scalar
            def _(sc):
                for half in range(2):
                    for m in range(4):
                        for c in range(2):
                            n = 8 * half + 2 * m + c
                            ps = psmm[n % 2]
                            sc.wait_ge(p.sem["x_hm"], b["x_hm"] + n + 1)
                            if half == 1 and m == 0 and c == 0:
                                sc.wait_ge(p.sem["x_om"], b["x_om"] + 16)
                            p.inc(sc.activation(
                                B["e_hT"][:, 1024 * m + 512 * c:1024 * m + 512 * c + 512],
                                ps[:, 0:512], AF.Relu,
                                bias=B["m_fb1"][:, 4 * e + m:4 * e + m + 1]),
                                "x_he")

            @blk.vector
            def _(v):
                gv = B["m_g2"][:, :]
                for half in range(2):
                    for ts4 in range(8):
                        for dc in range(2):
                            n2 = 16 * half + 2 * ts4 + dc
                            ps = psO[n2 % 2]
                            v.wait_ge(p.sem["x_om"], b["x_om"] + n2 + 1)
                            tt = 8 * half + ts4
                            asl = acc[:, D * tt + 512 * dc:D * tt + 512 * dc + 512]
                            v.scalar_tensor_tensor(
                                asl, ps[:, 0:512], gv[:, 16 * tt + e:16 * tt + e + 1],
                                asl, ALU.mult, ALU.add)
                            p.inc(v.drain(), "x_oa")


def phase_sb1_final(p, X_in, XA, Y_out):
    nc = p.nc
    B = p.buf
    acc = B["e_acc"]
    xo, ft, tmp, junk, cols = B["xt2"], B["U16"], B["tmp_f32"], B["junk"], B["cols"]
    b = p.base("z_bm", "z_ba", "z_dx", "z_df", "z_sq", "z_vs", "z_cb", "z_st",
               "z_ad")
    with nc.Block() as blk:
        @blk.tensor
        def _(t):
            for i in range(NT):
                for dc in range(2):
                    n = 2 * i + dc
                    ps = [B["psA"], B["psB"]][n % 2]
                    if n >= 2:
                        t.wait_ge(p.sem["z_ba"], b["z_ba"] + n - 1)
                    ins = t.matmul(ps[:, 0:512],
                                   B["m_g2T"][0:16, 128 * i:128 * i + 128],
                                   B["m_b2bf"][0:16, 512 * dc:512 * dc + 512],
                                   start=True, stop=True)
                    p.inc(ins, "z_bm")

        @blk.sync
        def _(s):
            for i in range(NT):
                if i >= 2:
                    s.wait_ge(p.sem["z_st"], b["z_st"] + 16 * (i - 1))
                p.inc(s.dma_start(out=xo[:, (i % 2) * D:(i % 2) * D + D],
                                  in_=X_in[128 * i:128 * i + 128, :]), "z_dx", 16)
                if i >= 2:
                    s.wait_ge(p.sem["z_cb"], b["z_cb"] + i - 1)
                p.inc(s.dma_start(out=ft[:, (i % 2) * D:(i % 2) * D + D],
                                  in_=XA[1 + 128 * i:129 + 128 * i, :]), "z_df", 16)
                if i >= 1:
                    s.wait_ge(p.sem["z_ad"], b["z_ad"] + i)
                    p.inc(s.dma_start(
                        out=Y_out[128 * (i - 1):128 * i, :],
                        in_=xo[:, ((i - 1) % 2) * D:((i - 1) % 2) * D + D]),
                        "z_st", 16)
            s.wait_ge(p.sem["z_ad"], b["z_ad"] + NT)
            p.inc(s.dma_start(out=Y_out[128 * (NT - 1):128 * NT, :],
                              in_=xo[:, ((NT - 1) % 2) * D:((NT - 1) % 2) * D + D]),
                  "z_st", 16)
            s.wait_ge(p.sem["z_st"], b["z_st"] + 16 * NT)

        @blk.vector
        def _(v):
            for i in range(NT):
                v.wait_ge(p.sem["z_df"], b["z_df"] + 16 * (i + 1))
                if i >= 1:
                    v.wait_ge(p.sem["z_cb"], b["z_cb"] + i)
                fti = ft[:, (i % 2) * D:(i % 2) * D + D]
                v.tensor_add(fti, fti, acc[:, D * i:D * i + D])
                for dc in range(2):
                    n = 2 * i + dc
                    ps = [B["psA"], B["psB"]][n % 2]
                    v.wait_ge(p.sem["z_bm"], b["z_bm"] + n + 1)
                    v.drain()
                    v.tensor_add(fti[:, 512 * dc:512 * dc + 512],
                                 fti[:, 512 * dc:512 * dc + 512], ps[:, 0:512])
                p.inc(v.drain(), "z_ba")
                p.inc(v.engine_nop(), "z_ba")
                v.drain()
                v.reduce_sum(cols[:, 0:1], fti, axis=AX.X)
                v.wait_ge(p.sem["z_sq"], b["z_sq"] + i + 1)
                v.drain()
                v.tensor_scalar_mul(cols[:, 2:3], cols[:, 0:1], 1.0 / D)
                v.drain()
                v.tensor_mul(cols[:, 3:4], cols[:, 2:3], cols[:, 2:3])
                v.tensor_scalar(cols[:, 4:5], cols[:, 1:2], 1.0 / D, None, ALU.mult)
                v.drain()
                v.tensor_sub(cols[:, 5:6], cols[:, 4:5], cols[:, 3:4])
                v.drain()
                v.tensor_scalar_add(cols[:, 5:6], cols[:, 5:6], 1e-5)
                v.drain()
                v.reciprocal(cols[:, 6:7], cols[:, 5:6])
                p.inc(v.drain(), "z_vs")
                v.wait_ge(p.sem["z_cb"], b["z_cb"] + i + 1)
                v.wait_ge(p.sem["z_dx"], b["z_dx"] + 16 * (i + 1))
                xoi = xo[:, (i % 2) * D:(i % 2) * D + D]
                v.tensor_add(xoi, xoi, tmp[:, :])
                p.inc(v.drain(), "z_ad")

        @blk.scalar
        def _(sc):
            for i in range(NT):
                sc.wait_ge(p.sem["z_ba"], b["z_ba"] + 2 * (i + 1))
                if i >= 1:
                    sc.wait_ge(p.sem["z_vs"], b["z_vs"] + i)
                fti = ft[:, (i % 2) * D:(i % 2) * D + D]
                sc.activation(junk[:, :], fti, AF.Square, accum_out=cols[:, 1:2])
                p.inc(sc.drain(), "z_sq")
                sc.wait_ge(p.sem["z_vs"], b["z_vs"] + i + 1)
                sc.activation(cols[:, 7:8], cols[:, 6:7], AF.Sqrt)
                sc.mul(cols[:, 8:9], cols[:, 2:3], -1.0)
                sc.drain()
                sc.activation(cols[:, 9:10], cols[:, 8:9], AF.Identity,
                              scale=cols[:, 7:8])
                sc.drain()
                if i >= 1:
                    sc.wait_ge(p.sem["z_ad"], b["z_ad"] + i)
                p.inc(sc.activation(tmp[:, :], fti, AF.Identity,
                                    scale=cols[:, 7:8], bias=cols[:, 9:10]),
                      "z_cb")


def emit_sb1(p, X_in, X_out, aps, scr):
    p.mark("1c")
    phase_sb1_consts(p, aps)
    p.mark("1lnh")
    phase_ln_to_hbm(p, X_in, scr["XA"])
    with p.scope():
        alloc_sb1_moa(p)
        p.buf["f_lgr"] = p.buf["m_l1g"]
        p.buf["f_lbr"] = p.buf["m_l1b"]
        with p.scope():
            alloc_sb1_lnT(p)
            p.mark("1lnT")
            phase_sb1_lnT_gates(p, scr["XA"], aps["attn_gate_w"],
                                p.buf["m_glog"], 8)
        p.mark("1kv")
        phase_sb1_kv(p)
        p.mark("1g")
        phase_sb1_gates(p, "moa")
        for qc in range(4):
            p.mark(f"1moa{qc}")
            phase_sb1_moa_qc(p, aps, qc, scr["XA"], scr["XB"],
                             (scr["RELP"], scr.get("RELP2", scr["RELP"])))
    with p.scope():
        alloc_sb1_moe(p)
        b = p.base("c_l2")
        with p.nc.Block() as blk:
            @blk.sync
            def _(s):
                p.inc(s.dma_start(out=p.buf["m_l2g"][:, :],
                                  in_=aps["ln2_w"].rearrange("(a d) -> a d", a=1).to_broadcast([128, D])),
                      "c_l2", 16)
                p.inc(s.dma_start(out=p.buf["m_l2b"][:, :],
                                  in_=aps["ln2_b"].rearrange("(a d) -> a d", a=1).to_broadcast([128, D])),
                      "c_l2", 16)
                s.wait_ge(p.sem["c_l2"], b["c_l2"] + 32)
        p.buf["f_lgr"] = p.buf["m_l2g"]
        p.buf["f_lbr"] = p.buf["m_l2b"]
        with p.scope():
            alloc_sb1_lnT(p)
            p.mark("1lnT2")
            phase_sb1_lnT_gates(p, scr["XB"], aps["ff_gate_w"],
                                p.buf["m_g2"], 16)
        p.mark("1g2")
        phase_sb1_gates(p, "ff")
        p.mark("1moe")
        phase_sb1_moe(p, aps, scr["XB"])
        p.mark("1fin")
        phase_sb1_final(p, X_in, scr["XB"], X_out)


def build_sb1_program():
    nc = bass.Bass("TRN2", target_bir_lowering=False, debug=False)
    X = nc.dram_tensor("X", [SEQ, D], F32, kind="ExternalInput").ap()
    aps = {}
    for nm, shp in (("ln1_w", [D]), ("ln1_b", [D]), ("ln2_w", [D]), ("ln2_b", [D]),
                    ("attn_gate_w", [D, 8]), ("q_w", [8, D, 128]),
                    ("qbT", [128, 8]), ("k_w", [D, 128]), ("k_b", [128]),
                    ("v_w", [D, 128]), ("v_b", [128]), ("o_w", [8, 128, D]),
                    ("o_b", [8, D]), ("rtT", [128, 129]),
                    ("ff_gate_w", [D, 16]), ("ff_w1", [16, D, 512]),
                    ("ffb1c", [128, 64]), ("ff_w2", [16, 512, D]),
                    ("ff_b2", [16, D])):
        aps[nm] = nc.dram_tensor(nm, shp, F32, kind="ExternalInput").ap()
    aps["C_EYE"] = nc.dram_tensor("C_EYE", [128, 128], F32, kind="ExternalInput").ap()
    Y = nc.dram_tensor("Y", [SEQ, D], F32, kind="ExternalOutput").ap()
    scr = {"XA": nc.dram_tensor("SCRXA", [SEQ + 1, D], F32, kind="Internal").ap(),
           "XB": nc.dram_tensor("SCRXB", [SEQ + 1, D], F32, kind="Internal").ap(),
           "RELP": nc.dram_tensor("SCRRELP", [SEQ, 1280], F32, kind="Internal").ap(),
           "RELP2": nc.dram_tensor("SCRRELP2", [SEQ, 1280], F32, kind="Internal").ap()}
    with ExitStack() as ctx:
        p = P(nc, ctx)
        alloc_common(p)
        alloc_sb1_outer(p)
        emit_sb1(p, X, Y, aps, scr)
    return nc


def sb1_host_prep(W):
    qbT = np.ascontiguousarray(W["L0_ffn_q_b"].T)
    rtT = np.ascontiguousarray(W["L0_ffn_rel_table"].T)
    b1 = W["L0_ffn_ff_b1"]
    ffb1c = np.ascontiguousarray(
        b1.reshape(16, 4, 128).transpose(2, 0, 1).reshape(128, 64))
    return qbT, rtT, ffb1c


def run_sb1_dev(x_seqs, W):
    n = x_seqs.shape[0]
    nc = build_sb1_program()
    qbT, rtT, ffb1c = sb1_host_prep(W)
    in_maps = []
    for i in range(n):
        im = {"X": np.ascontiguousarray(x_seqs[i]),
              "ln1_w": W["L0_ffn_ln1_w"], "ln1_b": W["L0_ffn_ln1_b"],
              "ln2_w": W["L0_ffn_ln2_w"], "ln2_b": W["L0_ffn_ln2_b"],
              "attn_gate_w": W["L0_ffn_attn_gate_w"], "q_w": W["L0_ffn_q_w"],
              "qbT": qbT, "k_w": W["L0_ffn_k_w"], "k_b": W["L0_ffn_k_b"],
              "v_w": W["L0_ffn_v_w"], "v_b": W["L0_ffn_v_b"],
              "o_w": W["L0_ffn_o_w"], "o_b": W["L0_ffn_o_b"], "rtT": rtT,
              "ff_gate_w": W["L0_ffn_ff_gate_w"], "ff_w1": W["L0_ffn_ff_w1"],
              "ffb1c": ffb1c, "ff_w2": W["L0_ffn_ff_w2"],
              "ff_b2": W["L0_ffn_ff_b2"],
              "C_EYE": np.eye(128, dtype=np.float32)}
        in_maps.append(im)
    res = run_bass_kernel_spmd(nc, in_maps, core_ids=list(range(n)))
    return np.stack([r["Y"] for r in res.results])


def phase_sb1_lnT_gates(p, Xp_src, gwf_ap, glog, ne, eps=1e-5):
    nc = p.nc
    B = p.buf
    hnT, xt, xn, junk, cols = B["xnT"], B["xt2"], B["xn2"], B["junk"], B["cols"]
    xaf, xf = B["m_xaf"], B["m_xf"]
    gwf = B["m_gwf"]
    pst = [B["psTA"], B["psTB"]]
    psF = B["psC"]
    psG = B["psZ"]
    b = p.base("n_dx", "n_vs", "n_sq", "n_nm", "n_af", "n_cb", "n_tt", "n_ev",
               "n_t2", "n_e2", "n_gm", "n_ge", "n_gw")
    hnT_v = hnT[:, :].rearrange("p (j s) -> p j s", j=ND, s=SEQ)

    with nc.Block() as blk:
        @blk.sync
        def _(s):
            p.inc(s.dma_start(out=gwf[:, 0:8 * ne].rearrange("p (a n) -> p a n", a=8),
                              in_=gwf_ap.rearrange("(a p) n -> p a n", p=128)),
                  "n_gw", 16)
            for i in range(NT):
                if i >= 2:
                    s.wait_ge(p.sem["n_nm"], b["n_nm"] + i - 1)
                p.inc(s.dma_start(out=xt[:, (i % 2) * D:(i % 2) * D + D],
                                  in_=Xp_src[1 + 128 * i:129 + 128 * i, :]),
                      "n_dx", 16)

        @blk.vector
        def _(v):
            for i in range(NT):
                if i >= 1:
                    v.wait_ge(p.sem["n_nm"], b["n_nm"] + i)
                v.wait_ge(p.sem["n_dx"], b["n_dx"] + 16 * (i + 1))
                xti = xt[:, (i % 2) * D:(i % 2) * D + D]
                v.reduce_sum(cols[:, 0:1], xti, axis=AX.X)
                v.wait_ge(p.sem["n_sq"], b["n_sq"] + i + 1)
                v.drain()
                v.tensor_scalar_mul(cols[:, 2:3], cols[:, 0:1], 1.0 / D)
                v.drain()
                v.tensor_mul(cols[:, 3:4], cols[:, 2:3], cols[:, 2:3])
                v.tensor_scalar(cols[:, 4:5], cols[:, 1:2], 1.0 / D, None, ALU.mult)
                v.drain()
                v.tensor_sub(cols[:, 5:6], cols[:, 4:5], cols[:, 3:4])
                v.drain()
                v.tensor_scalar_add(cols[:, 5:6], cols[:, 5:6], eps)
                v.drain()
                v.reciprocal(cols[:, 6:7], cols[:, 5:6])
                p.inc(v.drain(), "n_vs")
                v.wait_ge(p.sem["n_nm"], b["n_nm"] + i + 1)
                if i >= 1:
                    v.wait_ge(p.sem["n_t2"], b["n_t2"] + 2 * i)
                    v.wait_ge(p.sem["n_cb"], b["n_cb"] + i)
                v.tensor_mul(xaf[:, :], B["tmp_f32"][:, :], B["f_lgr"][:, :])
                v.drain()
                v.tensor_add(xaf[:, :], xaf[:, :], B["f_lbr"][:, :])
                p.inc(v.drain(), "n_af")
                if i >= 1:
                    v.wait_ge(p.sem["n_tt"], b["n_tt"] + i)
                    ps = pst[(i - 1) % 2]
                    src2 = ps[:, 0:ND * 128].rearrange("p (j c) -> p j c", j=ND, c=128)
                    p.inc(v.tensor_copy(hnT_v[:, :, 128 * (i - 1):128 * i], src2),
                          "n_ev")
            v.wait_ge(p.sem["n_tt"], b["n_tt"] + NT)
            ps = pst[(NT - 1) % 2]
            src2 = ps[:, 0:ND * 128].rearrange("p (j c) -> p j c", j=ND, c=128)
            p.inc(v.tensor_copy(hnT_v[:, :, 128 * (NT - 1):128 * NT], src2), "n_ev")

        @blk.scalar
        def _(sc):
            for i in range(NT):
                sc.wait_ge(p.sem["n_dx"], b["n_dx"] + 16 * (i + 1))
                if i >= 1:
                    sc.wait_ge(p.sem["n_vs"], b["n_vs"] + i)
                xti = xt[:, (i % 2) * D:(i % 2) * D + D]
                sc.activation(junk[:, :], xti, AF.Square, accum_out=cols[:, 1:2])
                p.inc(sc.drain(), "n_sq")
                sc.wait_ge(p.sem["n_vs"], b["n_vs"] + i + 1)
                sc.activation(cols[:, 7:8], cols[:, 6:7], AF.Sqrt)
                sc.mul(cols[:, 8:9], cols[:, 2:3], -1.0)
                sc.drain()
                sc.activation(cols[:, 9:10], cols[:, 8:9], AF.Identity,
                              scale=cols[:, 7:8])
                sc.drain()
                if i >= 1:
                    sc.wait_ge(p.sem["n_af"], b["n_af"] + i)
                p.inc(sc.activation(B["tmp_f32"][:, :], xti, AF.Identity,
                                    scale=cols[:, 7:8], bias=cols[:, 9:10]),
                      "n_nm")
                sc.wait_ge(p.sem["n_af"], b["n_af"] + i + 1)
                if i >= 2:
                    sc.wait_ge(p.sem["n_tt"], b["n_tt"] + i - 1)
                p.inc(sc.activation(xn[:, (i % 2) * D:(i % 2) * D + D],
                                    xaf[:, :], AF.Copy), "n_cb")
                if i >= 1:
                    sc.wait_ge(p.sem["n_gm"], b["n_gm"] + i)
                for rd in range(2):
                    sc.wait_ge(p.sem["n_t2"], b["n_t2"] + 2 * i + rd + 1)
                    p.inc(sc.activation(xf[:, 512 * rd:512 * rd + 512],
                                        psF[:, 0:512], AF.Copy), "n_e2")
                sc.wait_ge(p.sem["n_gm"], b["n_gm"] + i + 1)
                p.inc(sc.activation(glog[:, 16 * i:16 * i + ne], psG[:, 0:ne],
                                    AF.Copy), "n_ge")

        @blk.tensor
        def _(t):
            t.wait_ge(p.sem["n_gw"], b["n_gw"] + 16)
            for i in range(NT):
                t.wait_ge(p.sem["n_cb"], b["n_cb"] + i + 1)
                if i >= 2:
                    t.wait_ge(p.sem["n_ev"], b["n_ev"] + i - 1)
                ps = pst[i % 2]
                for j in range(ND):
                    ins = t.matmul(ps[:, 128 * j:128 * j + 128],
                                   xn[:, (i % 2) * D + 128 * j:
                                       (i % 2) * D + 128 * j + 128],
                                   B["idbf"][:, :], start=True, stop=True)
                    if j == ND - 1:
                        p.inc(ins, "n_tt")
                for rd in range(2):
                    if i > 0 or rd > 0:
                        t.wait_ge(p.sem["n_e2"], b["n_e2"] + 2 * i + rd)
                    if rd == 1:
                        pass
                    for j in range(4):
                        jj = 4 * rd + j
                        ins = t.matmul(psF[:, 128 * j:128 * j + 128],
                                       xaf[:, 128 * jj:128 * jj + 128],
                                       B["idf32b"][:, :], start=True, stop=True)
                        if j == 3:
                            p.inc(ins, "n_t2")
                t.wait_ge(p.sem["n_e2"], b["n_e2"] + 2 * i + 2)
                if i >= 1:
                    t.wait_ge(p.sem["n_ge"], b["n_ge"] + i)
                for kd in range(ND):
                    ins = t.matmul(psG[:, 0:ne],
                                   xf[:, 128 * kd:128 * kd + 128],
                                   gwf[:, ne * kd:ne * kd + ne],
                                   start=(kd == 0), stop=(kd == ND - 1))
                p.inc(ins, "n_gm")



def kernel_unfused(**inputs):
    W = {k: np.ascontiguousarray(np.asarray(v, dtype=np.float32))
         for k, v in inputs.items()}
    x = W["L0_attn_x"]
    nseq, ncore = 32, 8
    state = np.ascontiguousarray(x).copy()

    qbT, rtT, ffb1c = sb1_host_prep(W)
    w1p, b1c, w2p = sb3_pack_ffn(W["L1_ffn_w1"], W["L1_ffn_b1"], W["L1_ffn_w2"])
    c2 = sb2_consts()
    c3 = sb3_consts()
    eye = np.eye(128, dtype=np.float32)

    progs = [build_sb0_program(), build_sb1_program(), build_sb2_program(),
             build_sb3_program()]

    def immaker(sb, xs):
        if sb == 0:
            return {"X": xs, "Wq": W["L0_attn_Wq"], "Wk": W["L0_attn_Wk"],
                    "Wv": W["L0_attn_Wv"], "Wo": W["L0_attn_Wo"],
                    "lq1": W["L0_attn_lq1"], "lk1": W["L0_attn_lk1"],
                    "lq2": W["L0_attn_lq2"], "lk2": W["L0_attn_lk2"],
                    "ln_g": W["L0_attn_ln_g"], "ln_b": W["L0_attn_ln_b"],
                    "C_EYE": eye, "C_IOTA": c2["C_IOTA"], "C_IOTAC": c2["C_IOTAC"]}
        if sb == 1:
            return {"X": xs, "ln1_w": W["L0_ffn_ln1_w"], "ln1_b": W["L0_ffn_ln1_b"],
                    "ln2_w": W["L0_ffn_ln2_w"], "ln2_b": W["L0_ffn_ln2_b"],
                    "attn_gate_w": W["L0_ffn_attn_gate_w"], "q_w": W["L0_ffn_q_w"],
                    "qbT": qbT, "k_w": W["L0_ffn_k_w"], "k_b": W["L0_ffn_k_b"],
                    "v_w": W["L0_ffn_v_w"], "v_b": W["L0_ffn_v_b"],
                    "o_w": W["L0_ffn_o_w"], "o_b": W["L0_ffn_o_b"], "rtT": rtT,
                    "ff_gate_w": W["L0_ffn_ff_gate_w"], "ff_w1": W["L0_ffn_ff_w1"],
                    "ffb1c": ffb1c, "ff_w2": W["L0_ffn_ff_w2"],
                    "ff_b2": W["L0_ffn_ff_b2"], "C_EYE": eye}
        if sb == 2:
            im = {"X": xs, "W_kqv": W["L1_attn_W_kqv"],
                  "m": W["L1_attn_m"].reshape(16, 1, 1)}
            im.update(c2)
            return im
        im = {"X": xs, "wqkv": W["L1_ffn_wqkv"], "q_scale": W["L1_ffn_q_scale"],
              "k_scale": W["L1_ffn_k_scale"], "wo": W["L1_ffn_wo"],
              "ln_g": W["L1_ffn_ln_g"], "ln_b": W["L1_ffn_ln_b"],
              "w1p": w1p, "b1p": b1c, "w2p": w2p, "b2": W["L1_ffn_b2"]}
        im.update(c3)
        im["C_EYE"] = eye
        return im

    for sb in range(4):
        for s in range(4):
            seqs = [4 * c + s for c in range(ncore)]
            in_maps = [immaker(sb, np.ascontiguousarray(state[q])) for q in seqs]
            res = run_bass_kernel_spmd(progs[sb], in_maps,
                                       core_ids=list(range(ncore)))
            for c, q in enumerate(seqs):
                state[q] = res.results[c]["Y"]
    return state


LAST_MARKS = []


def build_fused_program():
    nc = bass.Bass("TRN2", target_bir_lowering=False, debug=False)
    X = nc.dram_tensor("X", [4 * SEQ, D], F32, kind="ExternalInput").ap()
    Y = nc.dram_tensor("Y", [4 * SEQ, D], F32, kind="ExternalOutput").ap()

    def dt(nm, shp):
        return nc.dram_tensor(nm, shp, F32, kind="ExternalInput").ap()

    aps0 = {nm: dt("a0_" + nm, shp) for nm, shp in (
        ("Wq", [D, D]), ("Wk", [D, D]), ("Wv", [D, D]), ("Wo", [D, D]),
        ("lq1", [64]), ("lk1", [64]), ("lq2", [64]), ("lk2", [64]),
        ("ln_g", [128]), ("ln_b", [128]))}
    aps1 = {nm: dt("a1_" + nm, shp) for nm, shp in (
        ("ln1_w", [D]), ("ln1_b", [D]), ("ln2_w", [D]), ("ln2_b", [D]),
        ("attn_gate_w", [D, 8]), ("q_w", [8, D, 128]), ("qbT", [128, 8]),
        ("k_w", [D, 128]), ("k_b", [128]), ("v_w", [D, 128]), ("v_b", [128]),
        ("o_w", [8, 128, D]), ("o_b", [8, D]), ("rtT", [128, 129]),
        ("ff_gate_w", [D, 16]), ("ff_w1", [16, D, 512]), ("ffb1c", [128, 64]),
        ("ff_w2", [16, 512, D]), ("ff_b2", [16, D]))}
    aps2 = {nm: dt("a2_" + nm, shp) for nm, shp in (
        ("W_kqv", [D, 3 * D]), ("m", [16, 1, 1]))}
    aps3 = {nm: dt("a3_" + nm, shp) for nm, shp in (
        ("wqkv", [2, D, 1536]), ("q_scale", [2, 64]), ("k_scale", [2, 64]),
        ("wo", [2, 512, D]), ("ln_g", [2, D]), ("ln_b", [2, D]),
        ("w1p", [2, D, 5632]), ("b1p", [2, 128, 44]), ("w2p", [2, 2816, D]),
        ("b2", [2, D]))}
    consts = {nm: dt(nm, shp) for nm, shp in (
        ("C_EYE", [128, 128]), ("C_IOTA", [1, SEQ]), ("C_IOTAC", [128, 1]),
        ("C_MASK", [128, 128]), ("C_CSQ", [SEQ, 64]), ("C_SNQ", [SEQ, 64]),
        ("C_CSK", [SEQ, 64]), ("C_SNK", [SEQ, 64]), ("C_MASKD", [128, 128]),
        ("C_MASKL", [128, 128]))}
    for d in (aps0, aps1, aps2, aps3):
        d.update(consts)

    XR1 = nc.dram_tensor("XR1", [SEQ, D], F32, kind="Internal").ap()
    XR2 = nc.dram_tensor("XR2", [SEQ, D], F32, kind="Internal").ap()
    S1A = nc.dram_tensor("S1A", [SEQ + 1, D], F32, kind="Internal").ap()
    S1B = nc.dram_tensor("S1B", [SEQ + 1, D], F32, kind="Internal").ap()
    S3A = nc.dram_tensor("S3A", [SEQ + 1, D], F32, kind="Internal").ap()
    S3B = nc.dram_tensor("S3B", [SEQ + 1, D], F32, kind="Internal").ap()
    S3C = nc.dram_tensor("S3C", [SEQ + 1, D], F32, kind="Internal").ap()
    RELP = nc.dram_tensor("RELP", [SEQ, 1280], F32, kind="Internal").ap()
    RELP2 = nc.dram_tensor("RELP2", [SEQ, 1280], F32, kind="Internal").ap()
    SCR = nc.dram_tensor("SCR", [64], F32, kind="Internal").ap()

    with ExitStack() as ctx:
        p = P(nc, ctx)
        alloc_common(p)
        for s in range(4):
            x_in = X[s * SEQ:(s + 1) * SEQ, :]
            y_out = Y[s * SEQ:(s + 1) * SEQ, :]
            with p.scope():
                alloc_sb0_extra(p)
                emit_sb0(p, x_in, XR1, aps0, SCR)
            with p.scope():
                alloc_sb1_outer(p)
                emit_sb1(p, XR1, XR2, aps1,
                         {"XA": S1A, "XB": S1B, "RELP": RELP,
                          "RELP2": RELP2})
            with p.scope():
                alloc_sb2_extra(p)
                emit_sb2(p, XR2, XR1, aps2)
            with p.scope():
                alloc_sb3_outer(p)
                emit_sb3(p, XR1, y_out, aps3,
                         {"XA": S3A, "XB": S3B, "XC": S3C})
        p.mark("end")
        global LAST_MARKS
        LAST_MARKS = list(p.marks)
    return nc


def kernel_fused(**inputs):
    W = {k: np.ascontiguousarray(np.asarray(v, dtype=np.float32))
         for k, v in inputs.items()}
    x = W["L0_attn_x"]
    qbT, rtT, ffb1c = sb1_host_prep(W)
    w1p, b1c, w2p = sb3_pack_ffn(W["L1_ffn_w1"], W["L1_ffn_b1"], W["L1_ffn_w2"])
    c2 = sb2_consts()
    c3 = sb3_consts()
    nc = build_fused_program()
    base = {
        "a0_Wq": W["L0_attn_Wq"], "a0_Wk": W["L0_attn_Wk"],
        "a0_Wv": W["L0_attn_Wv"], "a0_Wo": W["L0_attn_Wo"],
        "a0_lq1": W["L0_attn_lq1"], "a0_lk1": W["L0_attn_lk1"],
        "a0_lq2": W["L0_attn_lq2"], "a0_lk2": W["L0_attn_lk2"],
        "a0_ln_g": W["L0_attn_ln_g"], "a0_ln_b": W["L0_attn_ln_b"],
        "a1_ln1_w": W["L0_ffn_ln1_w"], "a1_ln1_b": W["L0_ffn_ln1_b"],
        "a1_ln2_w": W["L0_ffn_ln2_w"], "a1_ln2_b": W["L0_ffn_ln2_b"],
        "a1_attn_gate_w": W["L0_ffn_attn_gate_w"], "a1_q_w": W["L0_ffn_q_w"],
        "a1_qbT": qbT, "a1_k_w": W["L0_ffn_k_w"], "a1_k_b": W["L0_ffn_k_b"],
        "a1_v_w": W["L0_ffn_v_w"], "a1_v_b": W["L0_ffn_v_b"],
        "a1_o_w": W["L0_ffn_o_w"], "a1_o_b": W["L0_ffn_o_b"], "a1_rtT": rtT,
        "a1_ff_gate_w": W["L0_ffn_ff_gate_w"], "a1_ff_w1": W["L0_ffn_ff_w1"],
        "a1_ffb1c": ffb1c, "a1_ff_w2": W["L0_ffn_ff_w2"],
        "a1_ff_b2": W["L0_ffn_ff_b2"],
        "a2_W_kqv": W["L1_attn_W_kqv"], "a2_m": W["L1_attn_m"].reshape(16, 1, 1),
        "a3_wqkv": W["L1_ffn_wqkv"], "a3_q_scale": W["L1_ffn_q_scale"],
        "a3_k_scale": W["L1_ffn_k_scale"], "a3_wo": W["L1_ffn_wo"],
        "a3_ln_g": W["L1_ffn_ln_g"], "a3_ln_b": W["L1_ffn_ln_b"],
        "a3_w1p": w1p, "a3_b1p": b1c, "a3_w2p": w2p, "a3_b2": W["L1_ffn_b2"],
        "C_EYE": np.eye(128, dtype=np.float32),
    }
    base.update(c2)
    base.update(c3)
    in_maps = []
    for c in range(8):
        im = dict(base)
        im["X"] = np.ascontiguousarray(x[4 * c:4 * c + 4].reshape(4 * SEQ, D))
        in_maps.append(im)
    res = run_bass_kernel_spmd(nc, in_maps, core_ids=list(range(8)))
    out = np.empty_like(x)
    for c in range(8):
        out[4 * c:4 * c + 4] = res.results[c]["Y"].reshape(4, SEQ, D)
    return out


def kernel(**inputs):
    return kernel_fused(**inputs)
